# version 08
# baseline (speedup 1.0000x reference)
import numpy as np
import ml_dtypes
from contextlib import ExitStack

import concourse.bass as bass
import concourse.mybir as mybir
from concourse.bass_utils import run_bass_kernel_spmd
from concourse.alu_op_type import AluOpType as Op

F32 = mybir.dt.float32
BF16 = mybir.dt.bfloat16
I32 = mybir.dt.int32
AF = mybir.ActivationFunctionType
AX = mybir.AxisListType

D, L = 1024, 256
TP, TO = 4096, 2048
EPS = 1e-5
INVS8 = float(1.0 / np.sqrt(8.0))
SS = [2, 4, 6, 8]
OWN_H = [[0, 3, 4, 7], [1, 2, 5, 6]]
NEG = -1.0e30
MAGIC = 12582912.0


class Ctr:
    def __init__(self, nc, es, name):
        self.h = es.enter_context(nc.semaphore(name))
        self.n = 0

    def tick(self, instr, k=1):
        instr.then_inc(self.h, k)
        self.n += k
        return self.n


class DCtr:
    def __init__(self, nc, es, name):
        self.hw = es.enter_context(nc.semaphore(name + "_hw"))
        self.ann = es.enter_context(nc.semaphore(name + "_an"))
        self.n = 0
        self.sealed = 0

    @property
    def h(self):
        return self.ann

    def tick(self, instr, k=16):
        instr.then_inc(self.hw, 16)
        self.n += 1
        return self.n

    def seal(self, eng):
        if self.n > self.sealed:
            eng.wait_ge(self.hw, 16 * self.n)
            eng.nop().then_inc(self.ann, self.n - self.sealed)
            self.sealed = self.n


class MK:
    def __init__(self, debug=False, stop_after="all"):
        self.stop_after = stop_after
        self.debug = debug
        self.nc = bass.Bass("TRN2", target_bir_lowering=False, debug=False,
                            detect_race_conditions=False)
        self.dbg = {}
        self._ost_dma = 0

    def T(self, f):
        self.block.tensor(f)

    def V(self, f):
        self.block.vector(f)

    def S(self, f):
        def g(eng):
            f(eng)
            self.dS.seal(eng)
        self.block.scalar(g)

    def G(self, f):
        def g(eng):
            f(eng)
            self.dG.seal(eng)
        self.block.gpsimd(g)

    def Y(self, f):
        def g(eng):
            f(eng)
            self.dY.seal(eng)
        self.block.sync(g)

    def barrier(self):
        cs = [self.cT, self.cV, self.cS, self.cG]
        ds = [self.dY, self.dS, self.dG]
        vals = [(c.h, c.n) for c in cs] + [(d.ann, d.n) for d in ds]
        for emit in (self.T, self.V, self.S, self.G, self.Y):
            def _(e):
                for h, v in vals:
                    if v > 0:
                        e.wait_ge(h, v)
            emit(_)

    def build(self):
        nc = self.nc
        din = lambda n, s, dt=F32: nc.dram_tensor(n, s, dt, kind="ExternalInput")
        self.xb = din("xb", [TP, D])
        self.xo = din("xo", [TO, D])
        self.wq = din("wq", [D, D])
        self.wdkv = din("wdkv", [D, L])
        self.wuk = din("wuk", [L, D])
        self.wuv = din("wuv", [L, D])
        self.wo = din("wo", [D, D])
        self.rhsT = din("rhsT", [D, D])
        self.gamma2 = din("gamma2", [128, 2])
        self.beta2 = din("beta2", [128, 2])
        self.cosk = din("cosk", [128, TP])
        self.sink = din("sink", [128, TP])
        self.cosq = din("cosq", [128, TO])
        self.sinq = din("sinq", [128, TO])
        self.trimask = din("trimask", [128, 128], BF16)
        self.ident = din("ident", [128, 128], BF16)
        self.perm = din("perm", [128, 128], BF16)
        self.ones128 = din("ones128", [128, 1], BF16)
        self.ones1b = din("ones1b", [1, 128], BF16)
        self.ones1f = din("ones1f", [1, 128], F32)
        self.kill = din("kill", [1, 32])
        self.dko = din("dko", [1, 4], I32)
        self.vidx = din("vidx", [128, 16], mybir.dt.uint32)
        self.out = nc.dram_tensor("out", [TO, D], F32, kind="ExternalOutput")

        self.qt_d = nc.dram_tensor("qt_d", [128, 8 * TO], BF16)
        self.ql_d = nc.dram_tensor("ql_d", [128, 8 * TO], BF16)
        self.qr_d = nc.dram_tensor("qr_d", [128, 8 * D], BF16)
        self.rs_d = nc.dram_tensor("rs_d", [1, D], F32)
        self.kd_d = nc.dram_tensor("kd_d", [128, 2 * TP], BF16)
        self.vd_d = nc.dram_tensor("vd_d", [4096, 260], BF16)

        if self.debug:
            for nm, shp, dt in [("d_latnT", [128, 2 * TP], BF16),
                                ("d_qt", [128, 2 * TO], BF16),
                                ("d_ql", [128, 2048], BF16),
                                ("d_qr", [128, 8 * D], BF16),
                                ("d_kt", [128, 2 * TP], BF16),
                                ("d_vg", [128, 32 * 260], BF16),
                                ("d_ot", [128, 8 * TO], BF16),
                                ("d_rs", [1, D], F32),
                                ("d_ls", [128, 16], F32),
                                ("d_bias", [128, 32], F32),
                                ("d_qn", [1, 40], F32),
                                ("d_st", [128, D], F32),
                                ("d_gt", [128, D], F32),
                                ("d_gs", [128, D], F32),
                                ("d_RS", [128, D], F32),
                                ("d_ot2", [128, 8 * TO], BF16),
                                ("d_wob", [128, 8 * D], BF16),
                                ("d_rtf", [128, 2 * D], F32),
                                ("d_rssc", [128, 2], F32),
                                ("d_qrt", [128, D], F32)]:
                self.dbg[nm] = nc.dram_tensor(nm, shp, dt, kind="ExternalOutput")

        self.r_ko = nc.gpsimd.alloc_register(name="r_ko")
        self.r_vo = nc.gpsimd.alloc_register(name="r_vo")

        es = ExitStack()
        with es:
            sbuf = lambda n, shp, dt: es.enter_context(nc.sbuf_tensor(n, shp, dt))
            self.latnT = sbuf("latnT", [128, 2 * TP], BF16)
            self.otT = sbuf("otT", [128, 8 * TO], BF16)
            self.ls_all = sbuf("ls_all", [128, 16], F32)
            self.qn2m = sbuf("qn2m", [1, 40], F32)
            self.kn2m = sbuf("kn2m", [1, 24], F32)
            self.bias_all = sbuf("bias_all", [128, 32], F32)
            self.s_ident = sbuf("s_ident", [128, 128], BF16)
            self.s_perm = sbuf("s_perm", [128, 128], BF16)
            self.s_tri = sbuf("s_tri", [128, 128], BF16)
            self.s_ones128 = sbuf("s_ones128", [128, 1], BF16)
            self.s_ones1b = sbuf("s_ones1b", [1, 128], BF16)
            self.s_ones1f = sbuf("s_ones1f", [1, 128], F32)
            self.s_kill = sbuf("s_kill", [1, 32], F32)
            self.s_g2 = sbuf("s_g2", [128, 2], F32)
            self.s_b2 = sbuf("s_b2", [128, 2], F32)
            self.s_dko = sbuf("s_dko", [1, 4], I32)
            self.s_vidx = sbuf("s_vidx", [128, 16], mybir.dt.uint32)

            self.block = es.enter_context(nc.Block())
            self.cT = Ctr(nc, es, "cT")
            self.cV = Ctr(nc, es, "cV")
            self.cS = Ctr(nc, es, "cS")
            self.cG = Ctr(nc, es, "cG")
            self.dY = DCtr(nc, es, "dY")
            self.dS = DCtr(nc, es, "dS")
            self.dG = DCtr(nc, es, "dG")

            self.p0_consts()
            self.p0_rhs()
            self.barrier()
            if self.stop_after != "p0":
                self.p1()
                if self.stop_after in ("p1own", "p1ln"):
                    self.finish()
                    return nc
                self.barrier()
                self.latnT_ready = self.cS.n
                self.latnT_ready_v = self.cV.n
                if self.stop_after != "p1":
                    self.p2_groups()
                    self.barrier()
                    if self.stop_after != "p2":
                        self.p3_epilogue()
            self.finish()
        return nc

    def p0_consts(self):
        dY = self.dY
        pairs = [(self.s_ident, self.ident), (self.s_perm, self.perm),
                 (self.s_tri, self.trimask), (self.s_ones128, self.ones128),
                 (self.s_ones1b, self.ones1b), (self.s_ones1f, self.ones1f),
                 (self.s_kill, self.kill), (self.s_g2, self.gamma2),
                 (self.s_b2, self.beta2), (self.s_dko, self.dko),
                 (self.s_vidx, self.vidx)]
        def _(y):
            for dst, src in pairs:
                dY.tick(y.dma_start(dst[:], src.ap()), 16)
        self.Y(_)
        self.consts_ld = dY.n
        def _(v):
            v.memset(self.qn2m[:], 0.0)
            self.cV.tick(v.memset(self.kn2m[:], 0.0))
        self.V(_)

    def p0_rhs(self):
        nc, cV, cT, cS, dY, dG = self.nc, self.cV, self.cT, self.cS, self.dY, self.dG
        with ExitStack() as p0:
            sb = lambda n, shp, dt: p0.enter_context(nc.sbuf_tensor(n, shp, dt))
            rT_f = sb("rT_f", [128, 2 * D], F32)
            qrT_b = sb("qrT_b", [128, D], BF16)
            qr_i = sb("qr_i", [128, D], I32)
            qr_t = sb("qr_t", [128, D], F32)
            rssc = sb("rssc", [128, 2], F32)
            rsrow = sb("rsrow", [1, D], F32)
            qr_sb = sb("qr_sb", [128, 2 * D], BF16)
            qr_sgn = sb("qr_sgn", [128, D], F32)
            qr_ps = p0.enter_context(nc.psum_tensor("qr_ps", [128, 1024], BF16))

            ld_marks = []
            flat_mark = 0
            qr_evac_mark = 0
            def _(y):
                for c in range(2):
                    dY.tick(y.dma_start(rT_f[:, c * D:(c + 1) * D],
                                        self.rhsT.ap()[c * 128:(c + 1) * 128, :]), 16)
                    ld_marks.append(dY.n)
                if self.debug:
                    y.wait_ge(dY.hw, 16 * dY.n)
                    dY.tick(y.dma_start(self.dbg["d_rtf"].ap()[:, :], rT_f[:]), 16)
            self.Y(_)
            spill_marks = [0, 0]
            for c in range(8):
                buf = c % 2
                ld = ld_marks[c]
                def _(v):
                    v.wait_ge(dY.h, ld)
                    if flat_mark:
                        v.wait_ge(dG.h, flat_mark)
                    src = rT_f[:, buf * D:(buf + 1) * D]
                    v.reduce_max(rssc[:, 0:1], src, axis=AX.X, apply_absolute_value=True)
                    v.drain()
                    v.tensor_scalar(rssc[:, 0:1], rssc[:, 0:1], 1e-12, 1.0 / 127.0, Op.max, Op.mult)
                    v.drain()
                    v.reciprocal(rssc[:, 1:2], rssc[:, 0:1])
                    v.drain()
                    cV.tick(v.tensor_scalar(qr_t[:], src, rssc[:, 1:2], None, Op.mult))
                self.V(_)
                v_t = cV.n
                if self.debug and c == 0:
                    def _(y):
                        y.wait_ge(cV.h, v_t)
                        dY.tick(y.dma_start(self.dbg["d_rssc"].ap()[:, :], rssc[:]), 16)
                        dY.tick(y.dma_start(self.dbg["d_qrt"].ap()[:, :], qr_t[:]), 16)
                    self.Y(_)
                    dbg_m = dY.n
                    def _(v):
                        v.wait_ge(dY.h, dbg_m)
                        v.drain()
                        cV.tick(v.nop())
                    self.V(_)
                def _(v):
                    v.drain()
                    v.tensor_scalar(qr_t[:], qr_t[:], MAGIC, MAGIC, Op.add, Op.subtract)
                    v.drain()
                    v.tensor_copy(qr_i[:], qr_t[:])
                    v.drain()
                    cV.tick(v.tensor_copy(qrT_b[:], qr_i[:]))
                self.V(_)
                v_q = cV.n
                if c + 2 < 8:
                    def _(y):
                        y.wait_ge(cV.h, v_q)
                        dY.tick(y.dma_start(rT_f[:, buf * D:(buf + 1) * D],
                                            self.rhsT.ap()[(c + 2) * 128:(c + 3) * 128, :]), 16)
                        ld_marks.append(dY.n)
                    self.Y(_)
                def _(g):
                    g.wait_ge(cV.h, v_q)
                    dG.tick(g.dma_start(rsrow[0:1, c * 128:(c + 1) * 128], rssc[:, 0:1]), 16)
                self.G(_)
                flat_mark = dG.n
                s_prev = qr_evac_mark
                def _(t):
                    t.wait_ge(cV.h, v_q)
                    if s_prev:
                        t.wait_ge(cV.h, s_prev)
                    for f in range(8):
                        self._lmm = t.matmul(qr_ps[:, f * 128:(f + 1) * 128],
                                 qrT_b[:, f * 128:(f + 1) * 128], self.s_ident[:],
                                 start=True, stop=True, is_transpose=True,
                                 skip_group_check=True)
                    cT.tick(self._lmm)
                self.T(_)
                t_q = cT.n
                sp = spill_marks[buf]
                def _(v):
                    v.wait_ge(cT.h, t_q)
                    if sp:
                        v.wait_ge(dY.h, sp)
                    cV.tick(v.tensor_copy(qr_sb[:, buf * D:(buf + 1) * D], qr_ps[:]))
                self.V(_)
                qr_evac_mark = cV.n
                s_q = cV.n
                def _(y):
                    y.wait_ge(cV.h, s_q)
                    dY.tick(y.dma_start(
                        bass.AP(self.qr_d, c * 128, [[8 * D, 128], [D, 8], [1, 128]]),
                        qr_sb[:, buf * D:(buf + 1) * D]), 16)
                self.Y(_)
                spill_marks[buf] = dY.n
            def _(y):
                y.wait_ge(dG.h, dG.n)
                dY.tick(y.dma_start(self.rs_d.ap()[:], rsrow[:]), 16)
                if self.debug:
                    dY.tick(y.dma_start(self.dbg["d_rs"].ap()[:], rsrow[:]), 16)
                y.wait_ge(dY.hw, 16 * dY.n)
            self.Y(_)

    def emit_ln_chain(self, slot, src_ap, N, pre_waits):
        cV, cS, cG = self.cV, self.cS, self.cG
        L = self._lnS
        c0 = 12 * slot
        a2 = L[:, c0:c0 + 1]
        a1 = L[:, c0 + 1:c0 + 2]
        m = L[:, c0 + 2:c0 + 3]
        msq = L[:, c0 + 3:c0 + 4]
        teps = L[:, c0 + 4:c0 + 5]
        vpe = L[:, c0 + 5:c0 + 6]
        iv = L[:, c0 + 6:c0 + 7]
        rstd = L[:, c0 + 7:c0 + 8]
        bb = L[:, c0 + 8:c0 + 9]
        vmark, gmark = cV.n, cG.n
        def _(sc):
            for h, val in pre_waits:
                if val > 0:
                    sc.wait_ge(h, val)
            if vmark:
                sc.wait_ge(cV.h, vmark)
            if gmark:
                sc.wait_ge(cG.h, gmark)
            sc.activation(self._scrS[:, 0:N], src_ap, AF.Square, accum_out=a2)
            cS.tick(sc.activation(self._scrS[:, 0:N], src_ap, AF.Copy, accum_out=a1))
        self.S(_)
        s1 = cS.n
        def _(g):
            g.wait_ge(cS.h, s1)
            if vmark:
                g.wait_ge(cV.h, vmark)
            cG.tick(g.tensor_scalar(m, a1, 1.0 / N, None, Op.mult))
        self.G(_)
        g1 = cG.n
        def _(v):
            v.wait_ge(cG.h, g1)
            cV.tick(v.tensor_tensor(msq, m, m, Op.mult))
        self.V(_)
        v1 = cV.n
        def _(g):
            g.wait_ge(cV.h, v1)
            cG.tick(g.tensor_scalar(teps, msq, -1.0, EPS, Op.mult, Op.add))
        self.G(_)
        g2 = cG.n
        def _(sc):
            sc.wait_ge(cG.h, g2)
            scol = self._lnC[:, 1:2] if N == 1024 else self._lnC[:, 2:3]
            cS.tick(sc.activation(vpe, a2, AF.Identity, scale=scol, bias=teps))
        self.S(_)
        s2 = cS.n
        def _(v):
            v.wait_ge(cS.h, s2)
            cV.tick(v.reciprocal(iv, vpe))
        self.V(_)
        v2 = cV.n
        def _(sc):
            sc.wait_ge(cV.h, v2)
            cS.tick(sc.activation(rstd, iv, AF.Sqrt))
        self.S(_)
        s3 = cS.n
        def _(v):
            v.wait_ge(cS.h, s3)
            cV.tick(v.scalar_tensor_tensor(bb, m, -1.0, rstd, Op.mult, Op.mult))
        self.V(_)
        return s3, cV.n, m, rstd, bb

    def p1(self):
        nc = self.nc
        cT, cV, cS = self.cT, self.cV, self.cS
        dY, dS, dG = self.dY, self.dS, self.dG
        with ExitStack() as p1:
            sb = lambda n, shp, dt: p1.enter_context(nc.sbuf_tensor(n, shp, dt))
            xst = sb("xst", [128, 2 * D], F32)
            self._lnS = sb("lnS", [128, 36], F32)
            self._scrS = sb("scrS", [128, D], F32)
            self._lnC = sb("lnC", [128, 3], F32)
            xn_f = sb("xn_f", [128, D], F32)
            xnb = sb("xnb", [128, D], BF16)
            qtmp = sb("qtmp", [128, D], F32)
            qint = sb("qint", [128, D], I32)
            qlb = sb("qlb", [128, D], BF16)
            qsgn = sb("qsgn", [128, D], F32)
            qlev = sb("qlev", [128, D], BF16)
            xnTw = sb("xnTw", [128, 8 * 512], BF16)
            wdkv_b = sb("wdkv_b", [128, 8 * 256], BF16)
            cq = sb("cq", [128, TO], F32)
            sq = sb("sq", [128, TO], F32)
            qraw = sb("qraw", [128, 512], BF16)
            qsq = sb("qsq", [128, 512], BF16)
            t1b = sb("t1b", [128, 512], BF16)
            t2b = sb("t2b", [128, 512], BF16)
            qrot = sb("qrot", [128, 512], BF16)
            latnb = sb("latnb", [128, 256], BF16)
            wq_b = sb("wq_b", [128, 8 * D], BF16)

            with ExitStack() as pw:
                wq_f = pw.enter_context(nc.sbuf_tensor("wq_f", [128, 8 * D], F32))
                def _(v):
                    v.memset(self._lnC[:, 0:1], MAGIC)
                    v.memset(self._lnC[:, 1:2], 1.0 / 1024.0)
                    self.cV.tick(v.memset(self._lnC[:, 2:3], 1.0 / 256.0))
                self.V(_)
                def _(y):
                    dY.tick(y.dma_start(
                        bass.AP(wq_f, 0, [[8 * D, 128], [D, 8], [1, D]]),
                        bass.AP(self.wq, 0, [[D, 128], [128 * D, 8], [1, D]])), 16)
                    dY.tick(y.dma_start(cq[:], self.cosq.ap()), 16)
                    dY.tick(y.dma_start(sq[:], self.sinq.ap()), 16)
                self.Y(_)
                wq_ld = dY.n
                def _(g):
                    dG.tick(g.dma_start(
                        bass.AP(wdkv_b, 0, [[8 * 256, 128], [256, 8], [1, 256]]),
                        bass.AP(self.wdkv, 0, [[256, 128], [128 * 256, 8], [1, 256]])), 16)
                self.G(_)
                self.wdkv_ld = dG.n
                def _(v):
                    v.wait_ge(dY.h, wq_ld)
                    for f in range(8):
                        cV.tick(v.tensor_scalar(wq_b[:, f * D:(f + 1) * D],
                                                wq_f[:, f * D:(f + 1) * D],
                                                INVS8, None, Op.mult))
                self.V(_)
                wq_ready = cV.n
                def _(y):
                    y.wait_ge(cV.h, cV.n)
                self.Y(_)

            with ExitStack() as po:
                psx = lambda n, shp, dt: po.enter_context(nc.psum_tensor(n, shp, dt))
                xt_psA = psx("xt_psA", [128, 1024], BF16)
                xt_psB = psx("xt_psB", [128, 1024], BF16)
                q_ps = psx("q_ps", [128, 2 * 512], F32)
                qsw_ps = psx("qsw_ps", [128, 512], F32)
                nrm_ps = psx("nrm_ps", [1, 512], F32)

                x_lds = [0] * 16
                def _(y):
                    for t in range(2):
                        dY.tick(y.dma_start(xst[:, t * D:(t + 1) * D],
                                            self.xo.ap()[t * 128:(t + 1) * 128, :]), 16)
                        x_lds[t] = dY.n
                self.Y(_)

                qspill = 0
                qlspill = 0
                prev_xt_evac = 0
                prev_xt_evac_v = 0
                prev_perm = 0
                prev_tr = 0
                qps_free = [0, 0]
                for w in range(4):
                    for t in range(4):
                        ti = 4 * w + t
                        buf = ti % 2
                        srcx = xst[:, buf * D:(buf + 1) * D]
                        s3, vb, mA, rA, bA = self.emit_ln_chain(
                            0, srcx, 1024, [(dY.h, x_lds[ti]), (self.cT.h, prev_tr)])
                        mxa = self._lnS[:, 9:10]
                        ils = self._lnS[:, 10:11]
                        ptr = prev_tr
                        def _(sc):
                            sc.wait_ge(cV.h, vb)
                            if ptr:
                                sc.wait_ge(self.cT.h, ptr)
                            sc.activation(xn_f[:], srcx, AF.Identity, scale=rA, bias=bA)
                            cS.tick(sc.activation(xnb[:], srcx, AF.Identity, scale=rA, bias=bA))
                        self.S(_)
                        s_xnb = cS.n
                        gmark = self.cG.n
                        def _(v):
                            v.wait_ge(cS.h, s_xnb)
                            if gmark:
                                v.wait_ge(self.cG.h, gmark)
                            cV.tick(v.reduce_max(mxa, xn_f[:], axis=AX.X,
                                                 apply_absolute_value=True))
                        self.V(_)
                        v_mx = cV.n
                        def _(g):
                            g.wait_ge(cV.h, v_mx)
                            self.cG.tick(g.tensor_scalar(self.ls_all[:, ti:ti + 1], mxa,
                                                         1e-12, 1.0 / 127.0, Op.max, Op.mult))
                        self.G(_)
                        g_ls = self.cG.n
                        smark = cS.n
                        def _(v):
                            v.wait_ge(self.cG.h, g_ls)
                            if smark:
                                v.wait_ge(cS.h, smark)
                            cV.tick(v.reciprocal(ils, self.ls_all[:, ti:ti + 1]))
                        self.V(_)
                        v_il = cV.n
                        def _(sc):
                            sc.wait_ge(cV.h, v_il)
                            cS.tick(sc.activation(qtmp[:], xn_f[:], AF.Identity,
                                                  scale=ils, bias=self._lnC[:, 0:1]))
                        self.S(_)
                        s_q1 = cS.n
                        def _(sc):
                            sc.wait_ge(cS.h, s_q1)
                            cS.tick(sc.activation(qlb[:], qtmp[:], AF.Copy, bias=-MAGIC))
                        self.S(_)
                        s_qlb = cS.n
                        if ti + 2 < 16:
                            def _(y):
                                y.wait_ge(cS.h, s_xnb)
                                dY.tick(y.dma_start(xst[:, buf * D:(buf + 1) * D],
                                                    self.xo.ap()[(ti + 2) * 128:(ti + 3) * 128, :]), 16)
                                x_lds[ti + 2] = dY.n
                            self.Y(_)
                        pev = prev_xt_evac_v
                        def _(tns):
                            tns.wait_ge(cS.h, s_qlb)
                            if pev:
                                tns.wait_ge(cV.h, pev)
                            for cc in range(8):
                                self._lmm = tns.matmul(xt_psA[:, cc * 128:(cc + 1) * 128],
                                           xnb[:, cc * 128:(cc + 1) * 128], self.s_ident[:],
                                           start=True, stop=True, is_transpose=True,
                                           skip_group_check=True)
                                self._lmm = tns.matmul(xt_psB[:, cc * 128:(cc + 1) * 128],
                                           qlb[:, cc * 128:(cc + 1) * 128], self.s_ident[:],
                                           start=True, stop=True, is_transpose=True,
                                           skip_group_check=True)
                            cT.tick(self._lmm)
                        self.T(_)
                        prev_tr = cT.n
                        t_tr = cT.n
                        qls = qlspill
                        def _(v):
                            v.wait_ge(cT.h, t_tr)
                            if qls:
                                v.wait_ge(dY.h, qls)
                            v.tensor_copy(
                                bass.AP(xnTw, t * 128, [[4096, 128], [512, 8], [1, 128]]),
                                xt_psA[:])
                            cV.tick(v.tensor_copy(qlev[:], xt_psB[:]))
                        self.V(_)
                        xnT_evac_v = cV.n
                        prev_xt_evac = 0
                        prev_xt_evac_v = xnT_evac_v
                        s_ev = cV.n
                        def _(y):
                            y.wait_ge(cV.h, s_ev)
                            dY.tick(y.dma_start(
                                bass.AP(self.ql_d, ti * 128, [[8 * TO, 128], [TO, 8], [1, 128]]),
                                qlev[:]), 16)
                        self.Y(_)
                        qlspill = dY.n
                    if self.stop_after == "p1ln":
                        continue
                    xnT_ready_v = prev_xt_evac_v
                    for qd in range(8):
                        qbuf = qd % 2
                        qslice = q_ps[:, qbuf * 512:(qbuf + 1) * 512]
                        qf = qps_free[qbuf]
                        def _(tns):
                            tns.wait_ge(cV.h, max(wq_ready, xnT_ready_v))
                            if qf:
                                tns.wait_ge(cV.h, qf)
                            for f in range(8):
                                self._lmm = tns.matmul(qslice,
                                           wq_b[:, f * D + qd * 128:f * D + (qd + 1) * 128],
                                           xnTw[:, f * 512:(f + 1) * 512],
                                           start=(f == 0), stop=(f == 7))
                            cT.tick(self._lmm)
                        self.T(_)
                        t_q = cT.n
                        pp = prev_perm
                        def _(v):
                            v.wait_ge(cT.h, t_q)
                            if pp:
                                v.wait_ge(cT.h, pp)
                            cV.tick(v.tensor_copy(qraw[:], qslice))
                        self.V(_)
                        v_qraw = cV.n
                        qps_free[qbuf] = v_qraw
                        def _(tns):
                            tns.wait_ge(cV.h, v_qraw)
                            cT.tick(tns.matmul(qsw_ps[:], self.s_perm[:], qraw[:],
                                               start=True, stop=True))
                        self.T(_)
                        prev_perm = cT.n
                        t_sw = cT.n
                        qs = qspill
                        def _(v):
                            v.wait_ge(cT.h, t_sw)
                            if qs:
                                v.wait_ge(dY.h, qs)
                            v.tensor_tensor(t1b[:], qraw[:], cq[:, w * 512:(w + 1) * 512], Op.mult)
                            v.tensor_tensor(t2b[:], qsw_ps[:], sq[:, w * 512:(w + 1) * 512], Op.mult)
                            v.drain()
                            v.tensor_tensor(qrot[:], t1b[:], t2b[:], Op.add)
                            cV.tick(v.tensor_tensor(qsq[:], qraw[:], qraw[:], Op.mult))
                        self.V(_)
                        v_rope = cV.n
                        def _(tns):
                            tns.wait_ge(cV.h, v_rope)
                            cT.tick(tns.matmul(nrm_ps[:], self.s_ones128[:], qsq[:],
                                               start=True, stop=True))
                        self.T(_)
                        t_n = cT.n
                        def _(v):
                            v.wait_ge(cT.h, t_n)
                            cV.tick(v.reduce_max(self.qn2m[0:1, qd * 4 + w:qd * 4 + w + 1],
                                                 nrm_ps[:], axis=AX.X))
                        self.V(_)
                        def _(y):
                            y.wait_ge(cV.h, v_rope)
                            dY.tick(y.dma_start(
                                self.qt_d.ap()[:, qd * TO + w * 512:qd * TO + (w + 1) * 512],
                                qrot[:]), 16)
                        self.Y(_)
                        qspill = dY.n
                def _(v):
                    v.drain()
                    cV.tick(v.reduce_max(self.qn2m[0:1, 32:33], self.qn2m[0:1, 0:32], axis=AX.X))
                self.V(_)
                if self.debug:
                    def _(y):
                        y.wait_ge(cV.h, cV.n)
                        dY.tick(y.dma_start(self.dbg["d_ls"].ap()[:], self.ls_all[:]), 16)
                        dY.tick(y.dma_start(self.dbg["d_qn"].ap()[:], self.qn2m[:]), 16)
                    self.Y(_)
                self.barrier()

            if self.stop_after in ("p1own", "p1ln"):
                return
            with ExitStack() as pp2:
                psx = lambda n, shp, dt: pp2.enter_context(nc.psum_tensor(n, shp, dt))
                xt_ps2 = psx("xt_ps2", [128, 1024], BF16)
                lat_ps = psx("lat_ps", [128, 2 * 512], F32)
                lt_ps = psx("lt_ps", [128, 256], BF16)
                xnTt = pp2.enter_context(nc.sbuf_tensor("xnTt", [128, 2 * 1024], BF16))

                x_lds2 = [0] * 32
                def _(s):
                    for t in range(2):
                        dS.tick(s.dma_start(xst[:, t * D:(t + 1) * D],
                                            self.xb.ap()[t * 128:(t + 1) * 128, :]), 16)
                        x_lds2[t] = dS.n
                self.S(_)

                prev_xt_evac2 = 0
                prev_tr2 = 0
                prev_lat_free = [0, 0]
                prev_lt_evac = 0
                prev_lt_sevac = 0
                prev_lt_sevac_v = 0
                for ti in range(32):
                    buf = ti % 2
                    srcx = xst[:, buf * D:(buf + 1) * D]
                    s3, vb, mA, rA, bA = self.emit_ln_chain(
                        1, srcx, 1024, [(dS.h, x_lds2[ti]), (self.cT.h, prev_tr2)])
                    ptr2 = prev_tr2
                    def _(sc):
                        sc.wait_ge(cV.h, vb)
                        if ptr2:
                            sc.wait_ge(self.cT.h, ptr2)
                        cS.tick(sc.activation(xnb[:], srcx, AF.Identity, scale=rA, bias=bA))
                    self.S(_)
                    s_xnb2 = cS.n
                    if ti + 2 < 32:
                        def _(s):
                            s.wait_ge(cS.h, s_xnb2)
                            dS.tick(s.dma_start(xst[:, buf * D:(buf + 1) * D],
                                                self.xb.ap()[(ti + 2) * 128:(ti + 3) * 128, :]), 16)
                            x_lds2[ti + 2] = dS.n
                        self.S(_)
                    pe = prev_xt_evac2
                    def _(tns):
                        tns.wait_ge(cS.h, s_xnb2)
                        if pe:
                            tns.wait_ge(cV.h, pe)
                        for cc in range(8):
                            self._lmm = tns.matmul(xt_ps2[:, cc * 128:(cc + 1) * 128],
                                       xnb[:, cc * 128:(cc + 1) * 128], self.s_ident[:],
                                       start=True, stop=True, is_transpose=True,
                                       skip_group_check=True)
                        cT.tick(self._lmm)
                    self.T(_)
                    prev_tr2 = cT.n
                    t_tr = cT.n
                    tbuf = ti % 2
                    def _(v):
                        v.wait_ge(cT.h, t_tr)
                        cV.tick(v.tensor_copy(xnTt[:, tbuf * 1024:(tbuf + 1) * 1024], xt_ps2[:]))
                    self.V(_)
                    prev_xt_evac2 = cV.n
                    s_xnT = cV.n
                    plf = prev_lat_free[buf]
                    def _(tns):
                        tns.wait_ge(cV.h, s_xnT)
                        tns.wait_ge(dG.h, self.wdkv_ld)
                        if plf:
                            tns.wait_ge(cS.h, plf)
                        for cc in range(8):
                            self._lmm = tns.matmul(lat_ps[:, buf * 512:buf * 512 + 256],
                                       xnTt[:, tbuf * 1024 + cc * 128:tbuf * 1024 + (cc + 1) * 128],
                                       wdkv_b[:, cc * 256:(cc + 1) * 256],
                                       start=(cc == 0), stop=(cc == 7))
                        cT.tick(self._lmm)
                    self.T(_)
                    t_lat = cT.n
                    latsl = lat_ps[:, buf * 512:buf * 512 + 256]
                    s3b, vb2, mA2, rA2, bA2 = self.emit_ln_chain(
                        2, latsl, 256, [(cT.h, t_lat)])
                    plte = prev_lt_evac
                    def _(sc):
                        sc.wait_ge(cV.h, vb2)
                        if plte:
                            sc.wait_ge(cT.h, plte)
                        cS.tick(sc.activation(latnb[:], latsl, AF.Identity, scale=rA2, bias=bA2))
                    self.S(_)
                    prev_lat_free[buf] = cS.n
                    s_latn = cS.n
                    plse = prev_lt_sevac_v
                    def _(tns):
                        tns.wait_ge(cS.h, s_latn)
                        if plse:
                            tns.wait_ge(cV.h, plse)
                        for lc in range(2):
                            self._lmm = tns.matmul(lt_ps[:, lc * 128:(lc + 1) * 128],
                                       latnb[:, lc * 128:(lc + 1) * 128], self.s_ident[:],
                                       start=True, stop=True, is_transpose=True,
                                       skip_group_check=True)
                        cT.tick(self._lmm)
                    self.T(_)
                    prev_lt_evac = cT.n
                    t_lt = cT.n
                    def _(v):
                        v.wait_ge(cT.h, t_lt)
                        cV.tick(v.tensor_copy(
                            bass.AP(self.latnT, ti * 128, [[2 * TP, 128], [TP, 2], [1, 128]]),
                            lt_ps[:]))
                    self.V(_)
                    prev_lt_sevac_v = cV.n
                if self.debug:
                    def _(y):
                        y.wait_ge(cV.h, cV.n)
                        dY.tick(y.dma_start(self.dbg["d_latnT"].ap()[:], self.latnT[:]), 16)
                    self.Y(_)

    def p2_groups(self):
        nc = self.nc
        cV = self.cV
        with ExitStack() as p2:
            sb = lambda n, shp, dt: p2.enter_context(nc.sbuf_tensor(n, shp, dt))
            B = {}
            B["ktg"] = sb("ktg", [128, 2 * TP], BF16)
            B["vg"] = sb("vg", [128, 32 * 260], BF16)
            B["kdiag"] = sb("kdiag", [128, 2 * 2048], BF16)
            B["vdiag"] = sb("vdiag", [128, 16 * 260], BF16)
            B["qtg"] = sb("qtg", [128, 2 * TO], BF16)
            B["PT"] = sb("PT", [128, 2 * 2048], BF16)
            B["PTtri"] = sb("PTtri", [128, 2048], BF16)
            B["denb"] = sb("denb", [65, 8 * 512], F32)
            B["denm"] = sb("denm", [4, 2 * 512], F32)
            B["Dp"] = sb("Dp", [128, 4 * 512], F32)
            B["ostg"] = sb("ostg", [64, 512], BF16)
            B["wuk_b"] = sb("wuk_b", [128, 2 * 256], BF16)
            B["wuv_b"] = sb("wuv_b", [128, 2 * 256], BF16)
            B["wuk_f"] = sb("wuk_f", [128, 2 * 2 * 256], F32)
            B["wuv_f"] = sb("wuv_f", [128, 2 * 2 * 256], F32)
            B["gsc"] = sb("gsc", [128, 2], F32)
            B["bWk_sb"] = sb("bWk_sb", [128, 2], F32)
            B["bWv_sb"] = sb("bWv_sb", [128, 2], F32)
            B["bWvrow"] = sb("bWvrow", [1, 256], F32)
            B["bWvbc"] = sb("bWvbc", [128, 256], F32)
            B["ckt"] = sb("ckt", [128, 2 * 512], F32)
            B["skt"] = sb("skt", [128, 2 * 512], F32)
            B["kraw"] = sb("kraw", [128, 512], BF16)
            B["k1b"] = sb("k1b", [128, 512], BF16)
            B["k2b"] = sb("k2b", [128, 512], BF16)
            B["ksq"] = sb("ksq", [128, 512], BF16)
            B["bias_fb"] = sb("bias_fb", [1, 32], BF16)

            def _(v):
                cV.tick(v.memset(B["PTtri"][:], 0.0))
            self.V(_)
            def _(v):
                v.wait_ge(self.dY.h, self.consts_ld)
                cV.tick(v.tensor_scalar(B["gsc"][:], self.s_g2[:], INVS8, None, Op.mult))
            self.V(_)
            B["gsc_ready"] = cV.n

            self.w_mark = [None] * 4
            def _(sc):
                self.emit_wload(sc, 0, B, self.dS)
            self.S(_)

            for g in range(4):
                self.emit_group(g, B)

    def emit_pvp(self, tns, pd):
        cT, cV = self.cT, self.cV
        qb, hl = pd["qb"], pd["hl"]
        tns.wait_ge(self.dG.h, self._vdiag_done)
        for u in range(2):
            tns.wait_ge(cV.h, pd["tri_vt"][u])
            for j2 in range(2):
                jj = 2 * u + j2
                self._lmm = tns.matmul(
                    pd["pvsl"],
                    self._vdiag[:, (qb * 4 + jj) * 260 + hl * 65:
                                (qb * 4 + jj) * 260 + hl * 65 + 65],
                    self._PTtri[:, jj * 512:(jj + 1) * 512],
                    start=False, stop=(u == 1 and j2 == 1),
                    skip_group_check=True)
            cT.tick(self._lmm)

    def emit_wload(self, eng, g, B, d):
        par = g % 2
        d.tick(eng.dma_start(
            bass.AP(B["wuk_f"], par * 512, [[1024, 128], [256, 2], [1, 256]]),
            bass.AP(self.wuk, g * 256, [[D, 128], [128 * D, 2], [1, 256]])), 16)
        d.tick(eng.dma_start(
            bass.AP(B["wuv_f"], par * 512, [[1024, 128], [256, 2], [1, 256]]),
            bass.AP(self.wuv, g * 256, [[D, 128], [128 * D, 2], [1, 256]])), 16)
        self.w_mark[g] = (d, d.n)

    def emit_group(self, g, B):
        nc = self.nc
        cT, cV, cS, cG = self.cT, self.cV, self.cS, self.cG
        dY, dS, dG = self.dY, self.dS, self.dG
        ktg, vg, kdiag, vdiag, qtg = B["ktg"], B["vg"], B["kdiag"], B["vdiag"], B["qtg"]
        PT, PTtri, Dp, ostg = B["PT"], B["PTtri"], B["Dp"], B["ostg"]
        self._vdiag, self._PTtri = B["vdiag"], B["PTtri"]
        wuk_b, wuv_b, wuk_f, wuv_f = B["wuk_b"], B["wuv_b"], B["wuk_f"], B["wuv_f"]
        gsc, bWk_sb, bWv_sb = B["gsc"], B["bWk_sb"], B["bWv_sb"]
        bWvrow, bWvbc = B["bWvrow"], B["bWvbc"]
        ckt, skt = B["ckt"], B["skt"]
        kraw, k1b, k2b, ksq = B["kraw"], B["k1b"], B["k2b"], B["ksq"]
        bias_fb = B["bias_fb"]

        self.barrier()

        wpar = g % 2
        def _(y):
            y.wait_ge(cV.h, cV.n)
            dY.tick(y.dma_start(qtg[:], self.qt_d.ap()[:, 2 * g * TO:(2 * g + 2) * TO]), 16)
        self.Y(_)
        wld = dY.n
        wd, wm = self.w_mark[g]
        def _(v):
            v.wait_ge(wd.h, wm)
            v.wait_ge(cV.h, B["gsc_ready"])
            for lc in range(2):
                v.tensor_scalar(wuk_b[:, lc * 256:(lc + 1) * 256],
                                wuk_f[:, wpar * 512 + lc * 256:wpar * 512 + (lc + 1) * 256],
                                gsc[:, lc:lc + 1], None, Op.mult)
                cV.tick(v.tensor_scalar(wuv_b[:, lc * 256:(lc + 1) * 256],
                                        wuv_f[:, wpar * 512 + lc * 256:wpar * 512 + (lc + 1) * 256],
                                        self.s_g2[:, lc:lc + 1], None, Op.mult))
        self.V(_)
        w_ready = cV.n

        with ExitStack() as pb:
            bW_ps = pb.enter_context(nc.psum_tensor("bW_ps%d" % g, [128, 4 * 512], F32))
            bc_ps = pb.enter_context(nc.psum_tensor("bc_ps%d" % g, [128, 256], F32))
            def _(tns):
                tns.wait_ge(cV.h, w_ready)
                for half in range(2):
                    for lc in range(2):
                        self._lmm = tns.matmul(bW_ps[:, half * 512:half * 512 + 1],
                                   wuk_f[:, wpar * 512 + lc * 256 + half * 128:
                                         wpar * 512 + lc * 256 + (half + 1) * 128],
                                   self.s_b2[:, lc:lc + 1],
                                   start=(lc == 0), stop=(lc == 1))
                        self._lmm = tns.matmul(bW_ps[:, (2 + half) * 512:(2 + half) * 512 + 1],
                                   wuv_f[:, wpar * 512 + lc * 256 + half * 128:
                                         wpar * 512 + lc * 256 + (half + 1) * 128],
                                   self.s_b2[:, lc:lc + 1],
                                   start=(lc == 0), stop=(lc == 1))
                cT.tick(self._lmm)
            self.T(_)
            t_bw = cT.n
            def _(v):
                v.wait_ge(cT.h, t_bw)
                for half in range(2):
                    v.tensor_scalar(bWk_sb[:, half:half + 1],
                                    bW_ps[:, half * 512:half * 512 + 1],
                                    INVS8, None, Op.mult)
                    cV.tick(v.tensor_copy(bWv_sb[:, half:half + 1],
                                          bW_ps[:, (2 + half) * 512:(2 + half) * 512 + 1]))
            self.V(_)
            v_bw = cV.n
            def _(gp):
                gp.wait_ge(cV.h, v_bw)
                dG.tick(gp.dma_start(bWvrow[0:1, 0:128], bWv_sb[:, 0:1]), 16)
                dG.tick(gp.dma_start(bWvrow[0:1, 128:256], bWv_sb[:, 1:2]), 16)
            self.G(_)
            g_bw = dG.n
            def _(tns):
                tns.wait_ge(dG.h, g_bw)
                tns.wait_ge(dY.h, self.consts_ld)
                cT.tick(tns.matmul(bc_ps[:], self.s_ones1f[:], bWvrow[:],
                                   start=True, stop=True))
            self.T(_)
            t_bc = cT.n
            def _(v):
                v.wait_ge(cT.h, t_bc)
                cV.tick(v.tensor_copy(bWvbc[:], bc_ps[:]))
            self.V(_)
        bWvbc_ready = cV.n

        with ExitStack() as pk:
            psx = lambda n, shp, dt: pk.enter_context(nc.psum_tensor(n + str(g), shp, dt))
            kq_ps = psx("kq_ps", [128, 2 * 512], F32)
            ksw_ps = psx("ksw_ps", [128, 512], F32)
            nk_ps = psx("nk_ps", [1, 512], F32)
            vp_ps = psx("vp_ps", [128, 2 * 512], F32)

            def _(v):
                cV.tick(v.memset(bass.AP(vg, 64, [[32 * 260, 128], [260, 32], [65, 4]]), 1.0))
            self.V(_)
            ones_set = cV.n

            tab_ld = [0] * 8
            def _(y):
                for tb in range(2):
                    dY.tick(y.dma_start(ckt[:, tb * 512:(tb + 1) * 512],
                                        self.cosk.ap()[:, tb * 512:(tb + 1) * 512]), 16)
                    dY.tick(y.dma_start(skt[:, tb * 512:(tb + 1) * 512],
                                        self.sink.ap()[:, tb * 512:(tb + 1) * 512]), 16)
                    tab_ld[tb] = dY.n
            self.Y(_)

            prev_kraw_perm = 0
            prev_kq_free = [0, 0]
            for tb in range(8):
                tbuf = tb % 2
                for kc in range(2):
                    ksl = kq_ps[:, kc * 512:(kc + 1) * 512]
                    pkf = prev_kq_free[kc]
                    def _(tns):
                        tns.wait_ge(cV.h, max(w_ready, self.latnT_ready_v))
                        if pkf:
                            tns.wait_ge(cV.h, pkf)
                        for lc in range(2):
                            self._lmm = tns.matmul(ksl,
                                       wuk_b[:, lc * 256 + kc * 128:lc * 256 + (kc + 1) * 128],
                                       self.latnT[:, lc * TP + tb * 512:lc * TP + (tb + 1) * 512],
                                       start=(lc == 0), stop=(lc == 1))
                        cT.tick(self._lmm)
                    self.T(_)
                    t_k = cT.n
                    pkp = prev_kraw_perm
                    tl = tab_ld[tb]
                    def _(v):
                        v.wait_ge(cT.h, t_k)
                        v.wait_ge(dY.h, tl)
                        v.wait_ge(cV.h, bWvbc_ready)
                        if pkp:
                            v.wait_ge(cT.h, pkp)
                        cV.tick(v.tensor_scalar(kraw[:], ksl, bWk_sb[:, kc:kc + 1],
                                                None, Op.add))
                    self.V(_)
                    v_kraw = cV.n
                    prev_kq_free[kc] = v_kraw
                    def _(tns):
                        tns.wait_ge(cV.h, v_kraw)
                        cT.tick(tns.matmul(ksw_ps[:], self.s_perm[:], kraw[:],
                                           start=True, stop=True))
                    self.T(_)
                    prev_kraw_perm = cT.n
                    t_sw = cT.n
                    def _(v):
                        v.wait_ge(cT.h, t_sw)
                        cslice = ckt[:, tbuf * 512:(tbuf + 1) * 512]
                        sslice = skt[:, tbuf * 512:(tbuf + 1) * 512]
                        v.tensor_tensor(k1b[:], kraw[:], cslice, Op.mult)
                        v.tensor_tensor(k2b[:], ksw_ps[:], sslice, Op.mult)
                        v.drain()
                        v.tensor_tensor(ktg[:, kc * TP + tb * 512:kc * TP + (tb + 1) * 512],
                                        k1b[:], k2b[:], Op.add)
                        cV.tick(v.tensor_tensor(ksq[:], kraw[:], kraw[:], Op.mult))
                    self.V(_)
                    v_rope = cV.n
                    def _(tns):
                        tns.wait_ge(cV.h, v_rope)
                        cT.tick(tns.matmul(nk_ps[:], self.s_ones128[:], ksq[:],
                                           start=True, stop=True))
                    self.T(_)
                    t_n = cT.n
                    def _(v):
                        v.wait_ge(cT.h, t_n)
                        cV.tick(v.reduce_max(self.kn2m[0:1, kc * 8 + tb:kc * 8 + tb + 1],
                                             nk_ps[:], axis=AX.X))
                    self.V(_)
                if tb + 2 < 8:
                    vn = cV.n
                    def _(y):
                        y.wait_ge(cV.h, vn)
                        dY.tick(y.dma_start(ckt[:, tbuf * 512:(tbuf + 1) * 512],
                                            self.cosk.ap()[:, (tb + 2) * 512:(tb + 3) * 512]), 16)
                        dY.tick(y.dma_start(skt[:, tbuf * 512:(tbuf + 1) * 512],
                                            self.sink.ap()[:, (tb + 2) * 512:(tb + 3) * 512]), 16)
                        tab_ld[tb + 2] = dY.n
                    self.Y(_)
            prev_norm = cV.n

            def _(y):
                y.wait_ge(cV.h, prev_norm)
                dY.tick(y.dma_start(self.kd_d.ap()[:, :], ktg[:]), 16)
            self.Y(_)
            k_spill = dY.n
            from concourse.bass_core_rust import IndirectOffsetOnAxis
            def _(gp):
                gp.wait_ge(dY.h, k_spill)
                rko = self.r_ko
                for qb in range(4):
                    gp.reg_load(rko, self.s_dko[0:1, qb:qb + 1])
                    dG.tick(gp.dma_start(
                        bass.AP(kdiag, qb * 512, [[4096, 128], [2048, 2], [1, 512]]),
                        bass.AP(self.kd_d, rko, [[2 * TP, 128], [TP, 2], [1, 512]])), 16)
            self.G(_)
            kdiag_done = dG.n

            with ExitStack() as pbias:
                b_ps = pbias.enter_context(nc.psum_tensor("b_ps%d" % g, [128, 32], F32))
                def _(v):
                    v.wait_ge(cV.h, prev_norm)
                    v.drain()
                    v.reduce_max(self.kn2m[0:1, 16:17], self.kn2m[0:1, 0:16], axis=AX.X)
                    v.drain()
                    v.tensor_scalar(self.kn2m[0:1, 17:18], self.kn2m[0:1, 16:17],
                                    self.qn2m[0:1, 32:33], -0.5, Op.add, Op.mult)
                    v.drain()
                    cV.tick(v.tensor_scalar(bias_fb[:], self.s_kill[:],
                                            self.kn2m[0:1, 17:18], None, Op.add))
                self.V(_)
                v_bf = cV.n
                def _(tns):
                    tns.wait_ge(cV.h, v_bf)
                    cT.tick(tns.matmul(b_ps[:], self.s_ones1b[:], bias_fb[:],
                                       start=True, stop=True))
                self.T(_)
                t_b = cT.n
                def _(v):
                    v.wait_ge(cT.h, t_b)
                    cV.tick(v.tensor_copy(self.bias_all[:], b_ps[:]))
                self.V(_)
            bias_ready = cV.n

            prev_vevac = [0, 0]
            for ti in range(32):
                vbuf = ti % 2
                vsl = vp_ps[:, vbuf * 512:vbuf * 512 + 256]
                pve = prev_vevac[vbuf]
                def _(tns):
                    tns.wait_ge(cV.h, max(w_ready, self.latnT_ready_v))
                    if pve:
                        tns.wait_ge(cV.h, pve)
                    for lc in range(2):
                        self._lmm = tns.matmul(vsl,
                                   self.latnT[:, lc * TP + ti * 128:lc * TP + (ti + 1) * 128],
                                   wuv_b[:, lc * 256:(lc + 1) * 256],
                                   start=(lc == 0), stop=(lc == 1))
                    cT.tick(self._lmm)
                self.T(_)
                t_v = cT.n
                def _(v):
                    v.wait_ge(cT.h, t_v)
                    v.wait_ge(cV.h, max(bWvbc_ready, ones_set))
                    cV.tick(v.tensor_tensor(
                        bass.AP(vg, ti * 260, [[32 * 260, 128], [65, 4], [1, 64]]),
                        vsl, bWvbc[:], Op.add))
                self.V(_)
                prev_vevac[vbuf] = cV.n
            v_vdone = cV.n

            def _(y):
                y.wait_ge(cV.h, v_vdone)
                dY.tick(y.dma_start(
                    bass.AP(self.vd_d, 0, [[260, 128], [128 * 260, 32], [1, 260]]),
                    vg[:]), 16)
            self.Y(_)
            v_spill = dY.n
            def _(gp):
                gp.wait_ge(dY.h, v_spill)
                for t in range(16):
                    dG.tick(gp.indirect_dma_start(
                        vdiag[:, t * 260:(t + 1) * 260], None,
                        self.vd_d.ap()[:, :],
                        IndirectOffsetOnAxis(self.s_vidx[:, t:t + 1], 0)), 16)
            self.G(_)
            diag_done = dG.n
            self._vdiag_done = diag_done

        if self.debug and g == 0:
            def _(y):
                y.wait_ge(cV.h, cV.n)
                y.wait_ge(dG.h, dG.n)
                dY.tick(y.dma_start(self.dbg["d_kt"].ap()[:], ktg[:]), 16)
                dY.tick(y.dma_start(self.dbg["d_vg"].ap()[:], vg[:]), 16)
                dY.tick(y.dma_start(self.dbg["d_bias"].ap()[:], self.bias_all[:]), 16)
                dY.tick(y.dma_start(self.dbg["d_qt"].ap()[:], qtg[:]), 16)
            self.Y(_)

        with ExitStack() as pa:
            scb = [pa.enter_context(nc.psum_tensor("sc%d_%d" % (c, g), [128, 1024], F32))
                   for c in range(3)]
            pv_ps = pa.enter_context(nc.psum_tensor("pv_ps%d" % g, [128, 2 * 512], F32))
            denb, denm = B["denb"], B["denm"]
            pv_evac = [0, 0]
            ost_dma = self._ost_dma
            gather_mark = [0, 0]
            dp_free = [0, 0]
            pend = None
            norm_pend = None
            for qi, qb in enumerate([3, 2, 1, 0]):
                par = qb % 2
                nss = 2 * SS[qb]
                for hl in range(4):
                    kc, p = hl // 2, hl % 2
                    pr = p * 64
                    gc = 2 * g + kc
                    pvb = hl % 2
                    pvsl = pv_ps[0:65, pvb * 512:pvb * 512 + 512]
                    T0, S0, V0 = cT.n, cS.n, cV.n
                    if pend is not None:
                        assert T0 == pend["T0_next"], (T0, pend["T0_next"])
                    npend = 2 if pend is not None else 0
                    ops = [("sc", 0), ("sc", 1), ("sc", 2), ("pv", 0), ("sc", 3)]
                    if npend:
                        ops.append(("pvp", 0))
                    for i in range(1, nss - 2):
                        ops.append(("pv", i))
                        if i + 3 < nss:
                            ops.append(("sc", i + 3))
                    sc_tick = [0] * nss
                    pv_tick = [0] * nss
                    t = T0
                    for op, a in ops:
                        t += (2 if op == "pvp" else 1)
                        if op == "sc":
                            sc_tick[a] = t
                        elif op == "pv":
                            pv_tick[a] = t
                    e_tick = []
                    acc = S0
                    for i in range(nss):
                        acc += (2 if i >= nss - 2 else 1)
                        e_tick.append(acc)
                    tri_vt = [V0 + 1, V0 + 2]

                    pe0 = pv_evac[pvb]
                    pd = pend
                    def _(tns):
                        tns.wait_ge(cV.h, bias_ready)
                        tns.wait_ge(dY.h, wld)
                        if S0:
                            tns.wait_ge(cS.h, S0)
                        if pe0:
                            tns.wait_ge(cV.h, pe0)

                        def scores(i):
                            dst = scb[i % 3]
                            if i == nss - 2:
                                tns.wait_ge(dG.h, kdiag_done)
                            if i >= 3:
                                tns.wait_ge(cS.h, e_tick[i - 3])
                            for j2 in range(2):
                                j = 2 * i + j2
                                if i >= nss - 2:
                                    jj = j - 2 * (nss - 2)
                                    lhs = kdiag[pr:pr + 64,
                                                kc * 2048 + qb * 512 + jj * 128:
                                                kc * 2048 + qb * 512 + (jj + 1) * 128]
                                else:
                                    lhs = ktg[pr:pr + 64,
                                              kc * TP + j * 128:kc * TP + (j + 1) * 128]
                                self._lmm = tns.matmul(
                                    dst[:, j2 * 512:(j2 + 1) * 512], lhs,
                                    qtg[pr:pr + 64, kc * TO + qb * 512:kc * TO + (qb + 1) * 512],
                                    start=True, stop=True)
                            cT.tick(self._lmm)

                        def pv(i):
                            tns.wait_ge(cS.h, e_tick[i])
                            for j2 in range(2):
                                j = 2 * i + j2
                                src = PT[:, (i % 2) * 1024 + j2 * 512:
                                         (i % 2) * 1024 + (j2 + 1) * 512]
                                vsrc = vg[:, j * 260 + hl * 65:j * 260 + hl * 65 + 65]
                                self._lmm = tns.matmul(pvsl, vsrc, src,
                                                       start=(i == 0 and j2 == 0),
                                                       stop=False,
                                                       skip_group_check=True)
                            cT.tick(self._lmm)

                        for op, a in ops:
                            if op == "sc":
                                scores(a)
                            elif op == "pv":
                                pv(a)
                            else:
                                self.emit_pvp(tns, pd)
                    self.T(_)
                    assert cT.n == T0 + 2 * nss - 2 + npend, (cT.n, T0, nss)

                    def _(sc):
                        for i in range(nss):
                            sc.wait_ge(cT.h, sc_tick[i])
                            if i < nss - 2:
                                if i >= 2:
                                    sc.wait_ge(cT.h, pv_tick[i - 2])
                                bias_ap = self.bias_all[:, qb * 8 + i // 2:
                                                        qb * 8 + i // 2 + 1]
                                cS.tick(sc.activation(
                                    PT[:, (i % 2) * 1024:(i % 2) * 1024 + 1024],
                                    scb[i % 3][:], AF.Exp, bias=bias_ap))
                            else:
                                sc.wait_ge(cT.h, T0 + 5 + npend)
                                bias_ap = self.bias_all[:, qb * 8 + SS[qb] - 1:
                                                        qb * 8 + SS[qb]]
                                u = i - (nss - 2)
                                for j2 in range(2):
                                    jg = 2 * u + j2
                                    lo = jg * 512 + 128 * jg
                                    hi = (jg + 1) * 512
                                    cS.tick(sc.activation(
                                        PTtri[:, lo:hi],
                                        scb[i % 3][:, j2 * 512 + 128 * jg:(j2 + 1) * 512],
                                        AF.Exp, bias=bias_ap))
                    self.S(_)
                    assert cS.n == e_tick[-1]

                    is_last = (qi == 3 and hl == 3)
                    po = ost_dma
                    gm = gather_mark[par]
                    et0, et1 = e_tick[nss - 2], e_tick[nss - 1]
                    def _(v):
                        for u in range(2):
                            v.wait_ge(cS.h, (et0, et1)[u])
                            for j2 in range(2):
                                jg = 2 * u + j2
                                lo = jg * 512 + 128 * jg
                                ins = v.tensor_tensor(PTtri[:, lo:lo + 128],
                                                      PTtri[:, lo:lo + 128],
                                                      self.s_tri[:], Op.mult)
                            cV.tick(ins)
                        v.wait_ge(cT.h, cT.n + (2 if is_last else 7))
                        if gm:
                            v.wait_ge(dG.h, gm)
                        if p == 1 and po:
                            v.wait_ge(dY.h, po)
                        v.tensor_copy(denb[64:65, (par * 4 + hl) * 512:(par * 4 + hl + 1) * 512],
                                      pvsl[64:65, :])
                        if p == 0:
                            dstO = self.otT[0:64, gc * TO + qb * 512:gc * TO + (qb + 1) * 512]
                        else:
                            dstO = ostg[0:64, 0:512]
                        cV.tick(v.tensor_copy(dstO, pvsl[0:64, :]))
                    self.V(_)
                    assert cV.n == V0 + 3
                    pv_evac[pvb] = cV.n
                    pend = {"qb": qb, "hl": hl, "pvsl": pvsl, "tri_vt": tri_vt,
                            "T0_next": cT.n}
                    if p == 1:
                        v_o = cV.n
                        def _(y):
                            y.wait_ge(cV.h, v_o)
                            dY.tick(y.dma_start(
                                self.otT[64:128, gc * TO + qb * 512:gc * TO + (qb + 1) * 512],
                                ostg[0:64, 0:512]), 16)
                        self.Y(_)
                        ost_dma = dY.n
                        if qi == 0 and hl == 3 and g < 3:
                            def _(y):
                                self.emit_wload(y, g + 1, B, dY)
                            self.Y(_)
                    if hl == 0 and norm_pend is not None:
                        norm_pend()
                        norm_pend = None

                if qi == 3:
                    pd = pend
                    def _(tns):
                        self.emit_pvp(tns, pd)
                    self.T(_)
                    pend = None

                def emit_norm(qb=qb, par=par, v_den=cV.n, od2=ost_dma):
                    def _(gp):
                        gp.wait_ge(cV.h, v_den)
                        dG.tick(gp.dma_start(
                            denm[0:4, par * 512:(par + 1) * 512],
                            bass.AP(denb, 64 * 4096 + par * 2048,
                                    [[4096, 1], [512, 4], [1, 512]])), 16)
                    self.G(_)
                    gather_mark[par] = dG.n
                    gm2 = gather_mark[par]
                    def _(v):
                        v.wait_ge(dG.h, gm2)
                        cV.tick(v.reciprocal(denm[0:4, par * 512:(par + 1) * 512],
                                             denm[0:4, par * 512:(par + 1) * 512]))
                    self.V(_)
                    v_rec = cV.n
                    def _(gp):
                        gp.wait_ge(cV.h, v_rec)
                        for hl in range(4):
                            kc, p = hl // 2, hl % 2
                            dG.tick(gp.dma_start(
                                bass.AP(Dp, p * 64 * 2048 + (par * 2 + kc) * 512,
                                        [[2048, 64], [1, 512]]),
                                bass.AP(denm, hl * 1024 + par * 512,
                                        [[1024, 1], [0, 64], [1, 512]])), 16)
                    self.G(_)
                    def _(gp):
                        if od2:
                            gp.wait_ge(dY.h, od2)
                        for kc in range(2):
                            gc = 2 * g + kc
                            sl = self.otT[0:128,
                                          gc * TO + qb * 512:gc * TO + (qb + 1) * 512]
                            ins = gp.tensor_tensor(
                                sl, sl,
                                bass.AP(Dp, (par * 2 + kc) * 512, [[2048, 128], [1, 512]]),
                                Op.mult)
                        cG.tick(ins)
                    self.G(_)
                if qi == 3:
                    emit_norm()
                else:
                    norm_pend = emit_norm
            self._ost_dma = ost_dma

        if self.debug and g == 3:
            def _(y):
                y.wait_ge(cV.h, cV.n)
                y.wait_ge(self.cG.h, self.cG.n)
                y.wait_ge(dY.h, dY.n)
                dY.tick(y.dma_start(self.dbg["d_ot"].ap()[:], self.otT[:]), 16)
            self.Y(_)

    def p3_epilogue(self):
        nc = self.nc
        cT, cV, cS = self.cT, self.cV, self.cS
        dY, dS, dG = self.dY, self.dS, self.dG
        with ExitStack() as p3:
            sb = lambda n, shp, dt: p3.enter_context(nc.sbuf_tensor(n, shp, dt))
            ps = lambda n, shp, dt: p3.enter_context(nc.psum_tensor(n, shp, dt))
            wo_b = sb("wo_b", [128, 8 * D], BF16)
            qrh_b = sb("qrh_b", [128, 8 * D], BF16)
            rs_row = sb("rs_row", [1, D], F32)
            RS = sb("RS", [128, D], F32)
            qlt = sb("qlt", [128, 2 * 1024], BF16)
            xot = sb("xot", [128, 2 * D], F32)
            st_f = sb("st_f", [128, D], F32)
            gt_f = sb("gt_f", [128, D], F32)
            gs_f = sb("gs_f", [128, D], F32)
            outst = sb("outst", [128, 2 * D], F32)
            sp_ps = ps("sp_ps", [128, 1024], F32)
            ap_ps = ps("ap_ps", [128, 1024], F32)
            rs_ps = ps("rs_ps", [128, 1024], F32)

            def _(gp):
                dG.tick(gp.dma_start(
                    bass.AP(wo_b, 0, [[8 * D, 128], [D, 8], [1, D]]),
                    bass.AP(self.wo, 0, [[D, 128], [128 * D, 8], [1, D]])), 16)
            self.G(_)
            wo_ld = dG.n
            ql_lds = [0] * 16
            xo_lds = [0] * 16
            def _(y):
                dY.tick(y.dma_start(qrh_b[:], self.qr_d.ap()[:]), 16)
                dY.tick(y.dma_start(rs_row[:], self.rs_d.ap()[:]), 16)
                for t in range(2):
                    dY.tick(y.dma_start(qlt[:, t * 1024:(t + 1) * 1024],
                                        bass.AP(self.ql_d, t * 128,
                                                [[8 * TO, 128], [TO, 8], [1, 128]])), 16)
                    ql_lds[t] = dY.n
                    dY.tick(y.dma_start(xot[:, t * D:(t + 1) * D],
                                        self.xo.ap()[t * 128:(t + 1) * 128, :]), 16)
                    xo_lds[t] = dY.n
            self.Y(_)
            rs_ld = dY.n
            def _(tns):
                tns.wait_ge(dY.h, rs_ld)
                for band in range(2):
                    self._lmm = tns.matmul(rs_ps[:, band * 512:(band + 1) * 512], self.s_ones1f[:],
                               rs_row[0:1, band * 512:(band + 1) * 512],
                               start=True, stop=True)
                cT.tick(self._lmm)
            self.T(_)
            t_rs = cT.n
            def _(v):
                v.wait_ge(cT.h, t_rs)
                cV.tick(v.tensor_copy(RS[:], rs_ps[:]))
            self.V(_)
            rs_ready = cV.n

            if self.debug:
                def _(y):
                    y.wait_ge(dY.h, rs_ld)
                    y.wait_ge(dG.h, wo_ld)
                    dY.tick(y.dma_start(self.dbg["d_ot2"].ap()[:, :], self.otT[:]), 16)
                    dY.tick(y.dma_start(self.dbg["d_wob"].ap()[:, :], wo_b[:]), 16)
                self.Y(_)
            prev_ev = 0
            out_dma = [0, 0]
            prev_silu = 0
            for ti in range(16):
                buf = ti % 2
                qld, xld = ql_lds[ti], xo_lds[ti]
                pe = prev_ev
                def _(tns):
                    tns.wait_ge(dY.h, max(qld, xld))
                    tns.wait_ge(dG.h, wo_ld)
                    if pe:
                        tns.wait_ge(cV.h, pe)
                    for c in range(8):
                        for band in range(2):
                            self._lmm = tns.matmul(sp_ps[:, band * 512:(band + 1) * 512],
                                       self.otT[:, c * TO + ti * 128:c * TO + (ti + 1) * 128],
                                       wo_b[:, c * D + band * 512:c * D + (band + 1) * 512],
                                       start=(c == 0), stop=(c == 7), skip_group_check=True)
                    for c in range(8):
                        for band in range(2):
                            self._lmm = tns.matmul(ap_ps[:, band * 512:(band + 1) * 512],
                                       qlt[:, buf * 1024 + c * 128:buf * 1024 + (c + 1) * 128],
                                       qrh_b[:, c * D + band * 512:c * D + (band + 1) * 512],
                                       start=(c == 0), stop=(c == 7), skip_group_check=True)
                    cT.tick(self._lmm)
                self.T(_)
                t_mm = cT.n
                def _(v):
                    v.wait_ge(cT.h, t_mm)
                    v.wait_ge(cV.h, rs_ready)
                    if prev_silu:
                        v.wait_ge(cS.h, prev_silu)
                    v.tensor_copy(st_f[:], sp_ps[:])
                    v.tensor_tensor(gt_f[:], ap_ps[:], RS[:], Op.mult)
                    v.drain()
                    cV.tick(v.tensor_scalar(gt_f[:], gt_f[:], self.ls_all[:, ti:ti + 1],
                                            None, Op.mult))
                self.V(_)
                v_ev = cV.n
                prev_ev = v_ev
                if ti + 2 < 16:
                    def _(y):
                        y.wait_ge(cV.h, v_ev)
                        dY.tick(y.dma_start(qlt[:, buf * 1024:(buf + 1) * 1024],
                                            bass.AP(self.ql_d, (ti + 2) * 128,
                                                    [[8 * TO, 128], [TO, 8], [1, 128]])), 16)
                        ql_lds[ti + 2] = dY.n
                    self.Y(_)
                if self.debug and ti == 0:
                    def _(y):
                        y.wait_ge(cV.h, v_ev)
                        dY.tick(y.dma_start(self.dbg["d_st"].ap()[:, :], st_f[:]), 16)
                        dY.tick(y.dma_start(self.dbg["d_gt"].ap()[:, :], gt_f[:]), 16)
                        dY.tick(y.dma_start(self.dbg["d_RS"].ap()[:, :], RS[:]), 16)
                    self.Y(_)
                    dbg_mark = dY.n
                def _(sc):
                    sc.wait_ge(cV.h, v_ev)
                    if self.debug and ti == 0:
                        sc.wait_ge(dY.h, dbg_mark)
                    cS.tick(sc.activation(gs_f[:], gt_f[:], AF.Silu))
                self.S(_)
                s_si = cS.n
                prev_silu = s_si
                if self.debug and ti == 0:
                    def _(y):
                        y.wait_ge(cS.h, s_si)
                        dY.tick(y.dma_start(self.dbg["d_gs"].ap()[:, :], gs_f[:]), 16)
                    self.Y(_)
                    gs_mark = dY.n
                    def _(v):
                        v.wait_ge(dY.h, gs_mark)
                        v.drain()
                        cV.tick(v.nop())
                    self.V(_)
                od = out_dma[buf]
                def _(v):
                    v.wait_ge(cS.h, s_si)
                    if od:
                        v.wait_ge(dY.h, od)
                    v.tensor_tensor(gs_f[:], st_f[:], gs_f[:], Op.mult)
                    v.drain()
                    cV.tick(v.tensor_tensor(outst[:, buf * D:(buf + 1) * D],
                                            xot[:, buf * D:(buf + 1) * D], gs_f[:], Op.add))
                self.V(_)
                v_out = cV.n
                def _(y):
                    y.wait_ge(cV.h, v_out)
                    dY.tick(y.dma_start(self.out.ap()[ti * 128:(ti + 1) * 128, :],
                                        outst[:, buf * D:(buf + 1) * D]), 16)
                self.Y(_)
                out_dma[buf] = dY.n
                if ti + 2 < 16:
                    def _(y):
                        y.wait_ge(cV.h, v_out)
                        dY.tick(y.dma_start(xot[:, buf * D:(buf + 1) * D],
                                            self.xo.ap()[(ti + 2) * 128:(ti + 3) * 128, :]), 16)
                        xo_lds[ti + 2] = dY.n
                    self.Y(_)
            if self.debug:
                def _(y):
                    y.wait_ge(cV.h, cV.n)
                    dY.tick(y.dma_start(self.dbg["d_ql"].ap()[:, :], qlt[:]), 16)
                    dY.tick(y.dma_start(self.dbg["d_qr"].ap()[:], qrh_b[:]), 16)
                self.Y(_)
            self.barrier()

    def finish(self):
        def _(y):
            y.wait_ge(self.dY.h, self.dY.n)
        self.Y(_)
        def _(g):
            g.wait_ge(self.dG.h, self.dG.n)
        self.G(_)


def _host_consts():
    half = 32
    pos = np.arange(TP, dtype=np.float64)
    inv = 1.0 / (10000.0 ** (np.arange(half, dtype=np.float64) / half))
    ang = pos[None, :] * inv[:, None]
    cos = np.cos(ang)
    sin = np.sin(ang)
    cosk = np.tile(cos, (4, 1)).astype(np.float32)
    sgn = np.concatenate([-np.ones(32), np.ones(32)])
    sink = (np.tile(sin, (4, 1)) * np.tile(sgn, 2)[:, None]).astype(np.float32)
    tri = (np.arange(128)[:, None] <= np.arange(128)[None, :]).astype(ml_dtypes.bfloat16)
    ident = np.eye(128, dtype=ml_dtypes.bfloat16)
    perm = np.zeros((128, 128), np.float32)
    for i in range(128):
        blk, r = i // 64, i % 64
        perm[i, blk * 64 + ((r + 32) % 64)] = 1.0
    perm = perm.astype(ml_dtypes.bfloat16)
    return cosk, sink, tri, ident, perm


_CACHE = {}


def make_inmaps(x, wq, wdkv, wuk, wuv, wo, gam, bet, rhs):
    cosk, sink, tri, ident, perm = _host_consts()
    rhsT = np.ascontiguousarray(rhs.T)
    gamma2 = np.ascontiguousarray(gam.reshape(2, 128).T)
    beta2 = np.ascontiguousarray(bet.reshape(2, 128).T)
    in_maps = []
    for core in range(8):
        b, h = core // 2, core % 2
        own = OWN_H[h]
        xo = np.ascontiguousarray(
            np.concatenate([x[b, c * 512:(c + 1) * 512, :] for c in own], axis=0))
        vidx = np.zeros((128, 16), np.uint32)
        for qb in range(4):
            for j in range(4):
                vidx[:, qb * 4 + j] = own[qb] * 512 + j * 128 + np.arange(128)
        kl = np.zeros((1, 32), np.float32)
        for qb in range(4):
            for s in range(SS[qb] - 1):
                if s >= own[qb]:
                    kl[0, qb * 8 + s] = NEG
        in_maps.append({
            "xb": np.ascontiguousarray(x[b]), "xo": xo, "wq": wq, "wdkv": wdkv,
            "wuk": wuk, "wuv": wuv, "wo": wo, "rhsT": rhsT,
            "gamma2": gamma2, "beta2": beta2, "cosk": cosk, "sink": sink,
            "cosq": np.ascontiguousarray(
                np.concatenate([cosk[:, c * 512:(c + 1) * 512] for c in own], axis=1)),
            "sinq": np.ascontiguousarray(
                np.concatenate([sink[:, c * 512:(c + 1) * 512] for c in own], axis=1)),
            "trimask": tri, "ident": ident, "perm": perm,
            "ones128": np.ones((128, 1), ml_dtypes.bfloat16),
            "ones1b": np.ones((1, 128), ml_dtypes.bfloat16),
            "ones1f": np.ones((1, 128), np.float32),
            "kill": kl,
            "dko": np.array([[c * 512 for c in own]], np.int32),
            "vidx": vidx,
        })
    return in_maps


def kernel(L0_attn_x, L0_attn_W_q, L0_attn_W_dkv, L0_attn_W_uk, L0_attn_W_uv,
           L0_attn_W_o, L0_attn_ln_gamma, L0_attn_ln_beta, L0_ffn_rhs):
    x = np.ascontiguousarray(np.asarray(L0_attn_x, dtype=np.float32))
    wq = np.ascontiguousarray(np.asarray(L0_attn_W_q, np.float32))
    wdkv = np.ascontiguousarray(np.asarray(L0_attn_W_dkv, np.float32))
    wuk = np.ascontiguousarray(np.asarray(L0_attn_W_uk, np.float32))
    wuv = np.ascontiguousarray(np.asarray(L0_attn_W_uv, np.float32))
    wo = np.ascontiguousarray(np.asarray(L0_attn_W_o, np.float32))
    gam = np.asarray(L0_attn_ln_gamma, np.float32)
    bet = np.asarray(L0_attn_ln_beta, np.float32)
    rhs = np.ascontiguousarray(np.asarray(L0_ffn_rhs, np.float32))

    nc = _CACHE.get("nc")
    if nc is None:
        nc = MK(debug=False).build()
        _CACHE["nc"] = nc
    in_maps = make_inmaps(x, wq, wdkv, wuk, wuv, wo, gam, bet, rhs)
    res = run_bass_kernel_spmd(nc, in_maps, core_ids=list(range(8)))
    out = np.zeros((4, TP, D), np.float32)
    for core in range(8):
        b, h = core // 2, core % 2
        o = np.asarray(res.results[core]["out"])
        for i, c in enumerate(OWN_H[h]):
            out[b, c * 512:(c + 1) * 512, :] = o[i * 512:(i + 1) * 512, :]
    return out


# version 09 (final)
# speedup vs baseline: 1.4332x; 1.4332x over previous
import numpy as np
import ml_dtypes
from contextlib import ExitStack

import concourse.bass as bass
import concourse.mybir as mybir
from concourse.bass_utils import run_bass_kernel_spmd
from concourse.alu_op_type import AluOpType as Op

F32 = mybir.dt.float32
BF16 = mybir.dt.bfloat16
I32 = mybir.dt.int32
AF = mybir.ActivationFunctionType
AX = mybir.AxisListType

D, L = 1024, 256
TP, TO = 4096, 2048
EPS = 1e-5
INVS8 = float(1.0 / np.sqrt(8.0))
SS = [2, 4, 6, 8]
OWN_H = [[0, 3, 4, 7], [1, 2, 5, 6]]
NEG = -1.0e30
MAGIC = 12582912.0


class Ctr:
    def __init__(self, nc, es, name):
        self.h = es.enter_context(nc.semaphore(name))
        self.n = 0

    def tick(self, instr, k=1):
        instr.then_inc(self.h, k)
        self.n += k
        return self.n


class DCtr:
    def __init__(self, nc, es, name):
        self.hw = es.enter_context(nc.semaphore(name + "_hw"))
        self.ann = es.enter_context(nc.semaphore(name + "_an"))
        self.n = 0
        self.sealed = 0

    @property
    def h(self):
        return self.ann

    def tick(self, instr, k=16):
        instr.then_inc(self.hw, 16)
        self.n += 1
        return self.n

    def seal(self, eng):
        if self.n > self.sealed:
            eng.wait_ge(self.hw, 16 * self.n)
            eng.nop().then_inc(self.ann, self.n - self.sealed)
            self.sealed = self.n


class MK:
    def __init__(self, debug=False, stop_after="all"):
        self.stop_after = stop_after
        self.debug = debug
        self.nc = bass.Bass("TRN2", target_bir_lowering=False, debug=False,
                            detect_race_conditions=False)
        self.dbg = {}
        self._ost_dma = 0

    def T(self, f):
        self.block.tensor(f)

    def V(self, f):
        self.block.vector(f)

    def S(self, f):
        def g(eng):
            f(eng)
            self.dS.seal(eng)
        self.block.scalar(g)

    def G(self, f):
        def g(eng):
            f(eng)
            self.dG.seal(eng)
        self.block.gpsimd(g)

    def Y(self, f):
        def g(eng):
            f(eng)
            self.dY.seal(eng)
        self.block.sync(g)

    def barrier(self):
        cs = [self.cT, self.cV, self.cS, self.cG]
        ds = [self.dY, self.dS, self.dG]
        vals = [(c.h, c.n) for c in cs] + [(d.ann, d.n) for d in ds]
        for emit in (self.T, self.V, self.S, self.G, self.Y):
            def _(e):
                for h, v in vals:
                    if v > 0:
                        e.wait_ge(h, v)
            emit(_)

    def build(self):
        nc = self.nc
        din = lambda n, s, dt=F32: nc.dram_tensor(n, s, dt, kind="ExternalInput")
        self.xb = din("xb", [TP, D])
        self.xo = din("xo", [TO, D])
        self.wq = din("wq", [D, D])
        self.wdkv = din("wdkv", [D, L])
        self.wuk = din("wuk", [L, D])
        self.wuv = din("wuv", [L, D])
        self.wo = din("wo", [D, D])
        self.rhsT = din("rhsT", [D, D])
        self.gamma2 = din("gamma2", [128, 2])
        self.beta2 = din("beta2", [128, 2])
        self.cosk = din("cosk", [128, TP])
        self.sink = din("sink", [128, TP])
        self.cosq = din("cosq", [128, TO])
        self.sinq = din("sinq", [128, TO])
        self.trimask = din("trimask", [128, 128], BF16)
        self.ident = din("ident", [128, 128], BF16)
        self.perm = din("perm", [128, 128], BF16)
        self.ones128 = din("ones128", [128, 1], BF16)
        self.ones1b = din("ones1b", [1, 128], BF16)
        self.ones1f = din("ones1f", [1, 128], F32)
        self.kill = din("kill", [1, 32])
        self.dko = din("dko", [1, 4], I32)
        self.vidx = din("vidx", [128, 16], mybir.dt.uint32)
        self.out = nc.dram_tensor("out", [TO, D], F32, kind="ExternalOutput")

        self.qt_d = nc.dram_tensor("qt_d", [128, 8 * TO], BF16)
        self.ql_d = nc.dram_tensor("ql_d", [128, 8 * TO], BF16)
        self.qr_d = nc.dram_tensor("qr_d", [128, 8 * D], BF16)
        self.rs_d = nc.dram_tensor("rs_d", [1, D], F32)
        self.kd_d = nc.dram_tensor("kd_d", [128, 2 * TP], BF16)
        self.vd_d = nc.dram_tensor("vd_d", [4096, 260], BF16)

        if self.debug:
            for nm, shp, dt in [("d_latnT", [128, 2 * TP], BF16),
                                ("d_qt", [128, 2 * TO], BF16),
                                ("d_ql", [128, 2048], BF16),
                                ("d_qr", [128, 8 * D], BF16),
                                ("d_kt", [128, 2 * TP], BF16),
                                ("d_vg", [128, 32 * 260], BF16),
                                ("d_ot", [128, 8 * TO], BF16),
                                ("d_rs", [1, D], F32),
                                ("d_ls", [128, 16], F32),
                                ("d_bias", [128, 32], F32),
                                ("d_qn", [1, 40], F32),
                                ("d_st", [128, D], F32),
                                ("d_gt", [128, D], F32),
                                ("d_gs", [128, D], F32),
                                ("d_RS", [128, D], F32),
                                ("d_ot2", [128, 8 * TO], BF16),
                                ("d_wob", [128, 8 * D], BF16),
                                ("d_rtf", [128, 2 * D], F32),
                                ("d_rssc", [128, 2], F32),
                                ("d_qrt", [128, D], F32)]:
                self.dbg[nm] = nc.dram_tensor(nm, shp, dt, kind="ExternalOutput")

        self.r_ko = nc.gpsimd.alloc_register(name="r_ko")
        self.r_vo = nc.gpsimd.alloc_register(name="r_vo")

        es = ExitStack()
        with es:
            sbuf = lambda n, shp, dt: es.enter_context(nc.sbuf_tensor(n, shp, dt))
            self.latnT = sbuf("latnT", [128, 2 * TP], BF16)
            self.otT = sbuf("otT", [128, 8 * TO], BF16)
            self.ls_all = sbuf("ls_all", [128, 16], F32)
            self.qn2m = sbuf("qn2m", [1, 40], F32)
            self.kn2m = sbuf("kn2m", [1, 24], F32)
            self.bias_all = sbuf("bias_all", [128, 32], F32)
            self.s_ident = sbuf("s_ident", [128, 128], BF16)
            self.s_perm = sbuf("s_perm", [128, 128], BF16)
            self.s_tri = sbuf("s_tri", [128, 128], BF16)
            self.s_ones128 = sbuf("s_ones128", [128, 1], BF16)
            self.s_ones1b = sbuf("s_ones1b", [1, 128], BF16)
            self.s_ones1f = sbuf("s_ones1f", [1, 128], F32)
            self.s_kill = sbuf("s_kill", [1, 32], F32)
            self.s_g2 = sbuf("s_g2", [128, 2], F32)
            self.s_b2 = sbuf("s_b2", [128, 2], F32)
            self.s_dko = sbuf("s_dko", [1, 4], I32)
            self.s_vidx = sbuf("s_vidx", [128, 16], mybir.dt.uint32)

            self.block = es.enter_context(nc.Block())
            self.cT = Ctr(nc, es, "cT")
            self.cV = Ctr(nc, es, "cV")
            self.cS = Ctr(nc, es, "cS")
            self.cG = Ctr(nc, es, "cG")
            self.dY = DCtr(nc, es, "dY")
            self.dS = DCtr(nc, es, "dS")
            self.dG = DCtr(nc, es, "dG")

            self.p0_consts()
            self.p0_rhs()
            self.barrier()
            if self.stop_after != "p0":
                self.p1()
                if self.stop_after in ("p1own", "p1ln"):
                    self.finish()
                    return nc
                self.barrier()
                self.latnT_ready = self.cS.n
                self.latnT_ready_v = self.cV.n
                if self.stop_after != "p1":
                    self.p2_groups()
                    self.barrier()
                    if self.stop_after != "p2":
                        self.p3_epilogue()
            self.finish()
        return nc

    def p0_consts(self):
        dY = self.dY
        pairs = [(self.s_ident, self.ident), (self.s_perm, self.perm),
                 (self.s_tri, self.trimask), (self.s_ones128, self.ones128),
                 (self.s_ones1b, self.ones1b), (self.s_ones1f, self.ones1f),
                 (self.s_kill, self.kill), (self.s_g2, self.gamma2),
                 (self.s_b2, self.beta2), (self.s_dko, self.dko),
                 (self.s_vidx, self.vidx)]
        def _(y):
            for dst, src in pairs:
                dY.tick(y.dma_start(dst[:], src.ap()), 16)
        self.Y(_)
        self.consts_ld = dY.n
        def _(v):
            v.memset(self.qn2m[:], 0.0)
            self.cV.tick(v.memset(self.kn2m[:], 0.0))
        self.V(_)

    def p0_rhs(self):
        nc, cV, cT, cS, dY, dG = self.nc, self.cV, self.cT, self.cS, self.dY, self.dG
        with ExitStack() as p0:
            sb = lambda n, shp, dt: p0.enter_context(nc.sbuf_tensor(n, shp, dt))
            rT_f = sb("rT_f", [128, 2 * D], F32)
            qrT_b = sb("qrT_b", [128, D], BF16)
            qr_i = sb("qr_i", [128, D], I32)
            qr_t = sb("qr_t", [128, D], F32)
            rssc = sb("rssc", [128, 2], F32)
            rsrow = sb("rsrow", [1, D], F32)
            qr_sb = sb("qr_sb", [128, 2 * D], BF16)
            qr_sgn = sb("qr_sgn", [128, D], F32)
            qr_ps = p0.enter_context(nc.psum_tensor("qr_ps", [128, 1024], BF16))

            ld_marks = []
            flat_mark = 0
            qr_evac_mark = 0
            def _(y):
                for c in range(2):
                    dY.tick(y.dma_start(rT_f[:, c * D:(c + 1) * D],
                                        self.rhsT.ap()[c * 128:(c + 1) * 128, :]), 16)
                    ld_marks.append(dY.n)
                if self.debug:
                    y.wait_ge(dY.hw, 16 * dY.n)
                    dY.tick(y.dma_start(self.dbg["d_rtf"].ap()[:, :], rT_f[:]), 16)
            self.Y(_)
            spill_marks = [0, 0]
            for c in range(8):
                buf = c % 2
                ld = ld_marks[c]
                def _(v):
                    v.wait_ge(dY.h, ld)
                    if flat_mark:
                        v.wait_ge(dG.h, flat_mark)
                    src = rT_f[:, buf * D:(buf + 1) * D]
                    v.reduce_max(rssc[:, 0:1], src, axis=AX.X, apply_absolute_value=True)
                    v.drain()
                    v.tensor_scalar(rssc[:, 0:1], rssc[:, 0:1], 1e-12, 1.0 / 127.0, Op.max, Op.mult)
                    v.drain()
                    v.reciprocal(rssc[:, 1:2], rssc[:, 0:1])
                    v.drain()
                    cV.tick(v.tensor_scalar(qr_t[:], src, rssc[:, 1:2], None, Op.mult))
                self.V(_)
                v_t = cV.n
                if self.debug and c == 0:
                    def _(y):
                        y.wait_ge(cV.h, v_t)
                        dY.tick(y.dma_start(self.dbg["d_rssc"].ap()[:, :], rssc[:]), 16)
                        dY.tick(y.dma_start(self.dbg["d_qrt"].ap()[:, :], qr_t[:]), 16)
                    self.Y(_)
                    dbg_m = dY.n
                    def _(v):
                        v.wait_ge(dY.h, dbg_m)
                        v.drain()
                        cV.tick(v.nop())
                    self.V(_)
                def _(v):
                    v.drain()
                    v.tensor_scalar(qr_t[:], qr_t[:], MAGIC, MAGIC, Op.add, Op.subtract)
                    v.drain()
                    v.tensor_copy(qr_i[:], qr_t[:])
                    v.drain()
                    cV.tick(v.tensor_copy(qrT_b[:], qr_i[:]))
                self.V(_)
                v_q = cV.n
                if c + 2 < 8:
                    def _(y):
                        y.wait_ge(cV.h, v_q)
                        dY.tick(y.dma_start(rT_f[:, buf * D:(buf + 1) * D],
                                            self.rhsT.ap()[(c + 2) * 128:(c + 3) * 128, :]), 16)
                        ld_marks.append(dY.n)
                    self.Y(_)
                def _(g):
                    g.wait_ge(cV.h, v_q)
                    dG.tick(g.dma_start(rsrow[0:1, c * 128:(c + 1) * 128], rssc[:, 0:1]), 16)
                self.G(_)
                flat_mark = dG.n
                s_prev = qr_evac_mark
                def _(t):
                    t.wait_ge(cV.h, v_q)
                    if s_prev:
                        t.wait_ge(cV.h, s_prev)
                    for f in range(8):
                        self._lmm = t.matmul(qr_ps[:, f * 128:(f + 1) * 128],
                                 qrT_b[:, f * 128:(f + 1) * 128], self.s_ident[:],
                                 start=True, stop=True, is_transpose=True,
                                 skip_group_check=True)
                    cT.tick(self._lmm)
                self.T(_)
                t_q = cT.n
                sp = spill_marks[buf]
                def _(v):
                    v.wait_ge(cT.h, t_q)
                    if sp:
                        v.wait_ge(dY.h, sp)
                    cV.tick(v.tensor_copy(qr_sb[:, buf * D:(buf + 1) * D], qr_ps[:]))
                self.V(_)
                qr_evac_mark = cV.n
                s_q = cV.n
                def _(y):
                    y.wait_ge(cV.h, s_q)
                    dY.tick(y.dma_start(
                        bass.AP(self.qr_d, c * 128, [[8 * D, 128], [D, 8], [1, 128]]),
                        qr_sb[:, buf * D:(buf + 1) * D]), 16)
                self.Y(_)
                spill_marks[buf] = dY.n
            def _(y):
                y.wait_ge(dG.h, dG.n)
                dY.tick(y.dma_start(self.rs_d.ap()[:], rsrow[:]), 16)
                if self.debug:
                    dY.tick(y.dma_start(self.dbg["d_rs"].ap()[:], rsrow[:]), 16)
                y.wait_ge(dY.hw, 16 * dY.n)
            self.Y(_)

    def emit_ln_chain(self, mv6, mv, lnr, src_ap, n512, pre_waits_v):
        cV, cS = self.cV, self.cS
        def _(v):
            for h, val in pre_waits_v:
                if val > 0:
                    v.wait_ge(h, val)
            if n512 == 2:
                v.bn_stats(mv6[:, 0:6], src_ap[:, 0:512])
                v.bn_stats(mv6[:, 6:12], src_ap[:, 512:1024])
                v.drain()
                v.bn_aggr(mv[:], mv6[:])
            else:
                v.bn_stats(mv6[:, 0:6], src_ap)
                v.drain()
                v.bn_aggr(mv[:], mv6[:, 0:6])
            v.drain()
            v.tensor_scalar(lnr[:, 0:1], mv[:, 1:2], EPS, None, Op.add)
            v.drain()
            cV.tick(v.reciprocal(lnr[:, 1:2], lnr[:, 0:1]))
        self.V(_)
        v_st = cV.n
        def _(s):
            s.wait_ge(cV.h, v_st)
            cS.tick(s.activation(lnr[:, 2:3], lnr[:, 1:2], AF.Sqrt))
        self.S(_)
        return cS.n

    def p1(self):
        nc = self.nc
        cT, cV, cS = self.cT, self.cV, self.cS
        dY, dS, dG = self.dY, self.dS, self.dG
        with ExitStack() as p1:
            sb = lambda n, shp, dt: p1.enter_context(nc.sbuf_tensor(n, shp, dt))
            xst = sb("xst", [128, 2 * D], F32)
            mv6 = sb("mv6", [128, 12], F32)
            mv = sb("mv", [128, 2], F32)
            lnr = sb("lnr", [128, 3], F32)
            xn_f = sb("xn_f", [128, D], F32)
            xnb = sb("xnb", [128, D], BF16)
            qtmp = sb("qtmp", [128, D], F32)
            qint = sb("qint", [128, D], I32)
            qlb = sb("qlb", [128, D], BF16)
            qsgn = sb("qsgn", [128, D], F32)
            qlev = sb("qlev", [128, D], BF16)
            xnTw = sb("xnTw", [128, 8 * 512], BF16)
            wdkv_b = sb("wdkv_b", [128, 8 * 256], BF16)
            cq = sb("cq", [128, TO], F32)
            sq = sb("sq", [128, TO], F32)
            qraw = sb("qraw", [128, 512], BF16)
            qsq = sb("qsq", [128, 512], BF16)
            t1b = sb("t1b", [128, 512], BF16)
            t2b = sb("t2b", [128, 512], BF16)
            qrot = sb("qrot", [128, 512], BF16)
            latnb = sb("latnb", [128, 256], BF16)
            wq_b = sb("wq_b", [128, 8 * D], BF16)

            with ExitStack() as pw:
                wq_f = pw.enter_context(nc.sbuf_tensor("wq_f", [128, 8 * D], F32))
                def _(y):
                    dY.tick(y.dma_start(
                        bass.AP(wq_f, 0, [[8 * D, 128], [D, 8], [1, D]]),
                        bass.AP(self.wq, 0, [[D, 128], [128 * D, 8], [1, D]])), 16)
                    dY.tick(y.dma_start(cq[:], self.cosq.ap()), 16)
                    dY.tick(y.dma_start(sq[:], self.sinq.ap()), 16)
                self.Y(_)
                wq_ld = dY.n
                def _(g):
                    dG.tick(g.dma_start(
                        bass.AP(wdkv_b, 0, [[8 * 256, 128], [256, 8], [1, 256]]),
                        bass.AP(self.wdkv, 0, [[256, 128], [128 * 256, 8], [1, 256]])), 16)
                self.G(_)
                self.wdkv_ld = dG.n
                def _(v):
                    v.wait_ge(dY.h, wq_ld)
                    for f in range(8):
                        cV.tick(v.tensor_scalar(wq_b[:, f * D:(f + 1) * D],
                                                wq_f[:, f * D:(f + 1) * D],
                                                INVS8, None, Op.mult))
                self.V(_)
                wq_ready = cV.n
                def _(y):
                    y.wait_ge(cV.h, cV.n)
                self.Y(_)

            with ExitStack() as po:
                psx = lambda n, shp, dt: po.enter_context(nc.psum_tensor(n, shp, dt))
                xt_psA = psx("xt_psA", [128, 1024], BF16)
                xt_psB = psx("xt_psB", [128, 1024], BF16)
                q_ps = psx("q_ps", [128, 2 * 512], F32)
                qsw_ps = psx("qsw_ps", [128, 512], F32)
                nrm_ps = psx("nrm_ps", [1, 512], F32)

                x_lds = [0] * 16
                def _(y):
                    for t in range(2):
                        dY.tick(y.dma_start(xst[:, t * D:(t + 1) * D],
                                            self.xo.ap()[t * 128:(t + 1) * 128, :]), 16)
                        x_lds[t] = dY.n
                self.Y(_)

                qspill = 0
                qlspill = 0
                prev_xt_evac = 0
                prev_xt_evac_v = 0
                prev_perm = 0
                prev_tr = 0
                qps_free = [0, 0]
                for w in range(4):
                    for t in range(4):
                        ti = 4 * w + t
                        buf = ti % 2
                        srcx = xst[:, buf * D:(buf + 1) * D]
                        self.emit_ln_chain(mv6, mv, lnr, srcx, 2,
                                           [(dY.h, x_lds[ti]), (self.cT.h, prev_tr)])
                        s_r = cS.n
                        def _(v):
                            v.wait_ge(cS.h, s_r)
                            cV.tick(v.tensor_scalar(xn_f[:], srcx, mv[:, 0:1], lnr[:, 2:3],
                                                    Op.subtract, Op.mult))
                            v.drain()
                            v.tensor_copy(xnb[:], xn_f[:])
                            v.reduce_max(self.ls_all[:, ti:ti + 1], xn_f[:], axis=AX.X,
                                         apply_absolute_value=True)
                            v.drain()
                            v.tensor_scalar(self.ls_all[:, ti:ti + 1],
                                            self.ls_all[:, ti:ti + 1],
                                            1e-12, 1.0 / 127.0, Op.max, Op.mult)
                            v.drain()
                            v.reciprocal(lnr[:, 0:1], self.ls_all[:, ti:ti + 1])
                            v.drain()
                            cV.tick(v.tensor_scalar(qtmp[:], xn_f[:], lnr[:, 0:1], None, Op.mult))
                        self.V(_)
                        v_t = cV.n
                        def _(v):
                            v.drain()
                            v.tensor_scalar(qtmp[:], qtmp[:], MAGIC, MAGIC, Op.add, Op.subtract)
                            v.drain()
                            v.tensor_copy(qint[:], qtmp[:])
                            v.drain()
                            cV.tick(v.tensor_copy(qlb[:], qint[:]))
                        self.V(_)
                        v_done = cV.n
                        if ti + 2 < 16:
                            def _(y):
                                y.wait_ge(cV.h, v_done)
                                dY.tick(y.dma_start(xst[:, buf * D:(buf + 1) * D],
                                                    self.xo.ap()[(ti + 2) * 128:(ti + 3) * 128, :]), 16)
                                x_lds[ti + 2] = dY.n
                            self.Y(_)
                        pev = prev_xt_evac_v
                        def _(tns):
                            tns.wait_ge(cV.h, max(v_done, pev))
                            for cc in range(8):
                                self._lmm = tns.matmul(xt_psA[:, cc * 128:(cc + 1) * 128],
                                           xnb[:, cc * 128:(cc + 1) * 128], self.s_ident[:],
                                           start=True, stop=True, is_transpose=True,
                                           skip_group_check=True)
                                self._lmm = tns.matmul(xt_psB[:, cc * 128:(cc + 1) * 128],
                                           qlb[:, cc * 128:(cc + 1) * 128], self.s_ident[:],
                                           start=True, stop=True, is_transpose=True,
                                           skip_group_check=True)
                            cT.tick(self._lmm)
                        self.T(_)
                        prev_tr = cT.n
                        t_tr = cT.n
                        qls = qlspill
                        def _(v):
                            v.wait_ge(cT.h, t_tr)
                            if qls:
                                v.wait_ge(dY.h, qls)
                            v.tensor_copy(
                                bass.AP(xnTw, t * 128, [[4096, 128], [512, 8], [1, 128]]),
                                xt_psA[:])
                            cV.tick(v.tensor_copy(qlev[:], xt_psB[:]))
                        self.V(_)
                        xnT_evac_v = cV.n
                        prev_xt_evac = 0
                        prev_xt_evac_v = xnT_evac_v
                        s_ev = cV.n
                        def _(y):
                            y.wait_ge(cV.h, s_ev)
                            dY.tick(y.dma_start(
                                bass.AP(self.ql_d, ti * 128, [[8 * TO, 128], [TO, 8], [1, 128]]),
                                qlev[:]), 16)
                        self.Y(_)
                        qlspill = dY.n
                    if self.stop_after == "p1ln":
                        continue
                    xnT_ready_v = prev_xt_evac_v
                    for qd in range(8):
                        qbuf = qd % 2
                        qslice = q_ps[:, qbuf * 512:(qbuf + 1) * 512]
                        qf = qps_free[qbuf]
                        def _(tns):
                            tns.wait_ge(cV.h, max(wq_ready, xnT_ready_v))
                            if qf:
                                tns.wait_ge(cV.h, qf)
                            for f in range(8):
                                self._lmm = tns.matmul(qslice,
                                           wq_b[:, f * D + qd * 128:f * D + (qd + 1) * 128],
                                           xnTw[:, f * 512:(f + 1) * 512],
                                           start=(f == 0), stop=(f == 7))
                            cT.tick(self._lmm)
                        self.T(_)
                        t_q = cT.n
                        pp = prev_perm
                        def _(v):
                            v.wait_ge(cT.h, t_q)
                            if pp:
                                v.wait_ge(cT.h, pp)
                            cV.tick(v.tensor_copy(qraw[:], qslice))
                        self.V(_)
                        v_qraw = cV.n
                        qps_free[qbuf] = v_qraw
                        def _(tns):
                            tns.wait_ge(cV.h, v_qraw)
                            cT.tick(tns.matmul(qsw_ps[:], self.s_perm[:], qraw[:],
                                               start=True, stop=True))
                        self.T(_)
                        prev_perm = cT.n
                        t_sw = cT.n
                        qs = qspill
                        def _(v):
                            v.wait_ge(cT.h, t_sw)
                            if qs:
                                v.wait_ge(dY.h, qs)
                            v.tensor_tensor(t1b[:], qraw[:], cq[:, w * 512:(w + 1) * 512], Op.mult)
                            v.tensor_tensor(t2b[:], qsw_ps[:], sq[:, w * 512:(w + 1) * 512], Op.mult)
                            v.drain()
                            v.tensor_tensor(qrot[:], t1b[:], t2b[:], Op.add)
                            cV.tick(v.tensor_tensor(qsq[:], qraw[:], qraw[:], Op.mult))
                        self.V(_)
                        v_rope = cV.n
                        def _(tns):
                            tns.wait_ge(cV.h, v_rope)
                            cT.tick(tns.matmul(nrm_ps[:], self.s_ones128[:], qsq[:],
                                               start=True, stop=True))
                        self.T(_)
                        t_n = cT.n
                        def _(v):
                            v.wait_ge(cT.h, t_n)
                            cV.tick(v.reduce_max(self.qn2m[0:1, qd * 4 + w:qd * 4 + w + 1],
                                                 nrm_ps[:], axis=AX.X))
                        self.V(_)
                        def _(y):
                            y.wait_ge(cV.h, v_rope)
                            dY.tick(y.dma_start(
                                self.qt_d.ap()[:, qd * TO + w * 512:qd * TO + (w + 1) * 512],
                                qrot[:]), 16)
                        self.Y(_)
                        qspill = dY.n
                def _(v):
                    v.drain()
                    cV.tick(v.reduce_max(self.qn2m[0:1, 32:33], self.qn2m[0:1, 0:32], axis=AX.X))
                self.V(_)
                if self.debug:
                    def _(y):
                        y.wait_ge(cV.h, cV.n)
                        dY.tick(y.dma_start(self.dbg["d_ls"].ap()[:], self.ls_all[:]), 16)
                        dY.tick(y.dma_start(self.dbg["d_qn"].ap()[:], self.qn2m[:]), 16)
                    self.Y(_)
                self.barrier()

            if self.stop_after in ("p1own", "p1ln"):
                return
            with ExitStack() as pp2:
                psx = lambda n, shp, dt: pp2.enter_context(nc.psum_tensor(n, shp, dt))
                xt_ps2 = psx("xt_ps2", [128, 1024], BF16)
                lat_ps = psx("lat_ps", [128, 2 * 512], F32)
                lt_ps = psx("lt_ps", [128, 256], BF16)
                xnTt = pp2.enter_context(nc.sbuf_tensor("xnTt", [128, 2 * 1024], BF16))

                x_lds2 = [0] * 32
                def _(s):
                    for t in range(2):
                        dS.tick(s.dma_start(xst[:, t * D:(t + 1) * D],
                                            self.xb.ap()[t * 128:(t + 1) * 128, :]), 16)
                        x_lds2[t] = dS.n
                self.S(_)

                prev_xt_evac2 = 0
                prev_tr2 = 0
                prev_lat_free = [0, 0]
                prev_lt_evac = 0
                prev_lt_sevac = 0
                prev_lt_sevac_v = 0
                for ti in range(32):
                    buf = ti % 2
                    srcx = xst[:, buf * D:(buf + 1) * D]
                    self.emit_ln_chain(mv6, mv, lnr, srcx, 2,
                                       [(dS.h, x_lds2[ti]), (self.cT.h, prev_tr2)])
                    s_r = cS.n
                    def _(v):
                        v.wait_ge(cS.h, s_r)
                        cV.tick(v.tensor_scalar(xnb[:], srcx, mv[:, 0:1], lnr[:, 2:3],
                                                Op.subtract, Op.mult))
                    self.V(_)
                    v_xnb = cV.n
                    if ti + 2 < 32:
                        def _(s):
                            s.wait_ge(cV.h, v_xnb)
                            dS.tick(s.dma_start(xst[:, buf * D:(buf + 1) * D],
                                                self.xb.ap()[(ti + 2) * 128:(ti + 3) * 128, :]), 16)
                            x_lds2[ti + 2] = dS.n
                        self.S(_)
                    pe = prev_xt_evac2
                    def _(tns):
                        tns.wait_ge(cV.h, max(v_xnb, pe))
                        for cc in range(8):
                            self._lmm = tns.matmul(xt_ps2[:, cc * 128:(cc + 1) * 128],
                                       xnb[:, cc * 128:(cc + 1) * 128], self.s_ident[:],
                                       start=True, stop=True, is_transpose=True,
                                       skip_group_check=True)
                        cT.tick(self._lmm)
                    self.T(_)
                    prev_tr2 = cT.n
                    t_tr = cT.n
                    tbuf = ti % 2
                    def _(v):
                        v.wait_ge(cT.h, t_tr)
                        cV.tick(v.tensor_copy(xnTt[:, tbuf * 1024:(tbuf + 1) * 1024], xt_ps2[:]))
                    self.V(_)
                    prev_xt_evac2 = cV.n
                    s_xnT = cV.n
                    plf = prev_lat_free[buf]
                    def _(tns):
                        tns.wait_ge(cV.h, s_xnT)
                        tns.wait_ge(dG.h, self.wdkv_ld)
                        if plf:
                            tns.wait_ge(cV.h, plf)
                        for cc in range(8):
                            self._lmm = tns.matmul(lat_ps[:, buf * 512:buf * 512 + 256],
                                       xnTt[:, tbuf * 1024 + cc * 128:tbuf * 1024 + (cc + 1) * 128],
                                       wdkv_b[:, cc * 256:(cc + 1) * 256],
                                       start=(cc == 0), stop=(cc == 7))
                        cT.tick(self._lmm)
                    self.T(_)
                    t_lat = cT.n
                    latsl = lat_ps[:, buf * 512:buf * 512 + 256]
                    self.emit_ln_chain(mv6, mv, lnr, latsl, 1, [(cT.h, t_lat)])
                    s_r2 = cS.n
                    plte = prev_lt_evac
                    def _(v):
                        v.wait_ge(cS.h, s_r2)
                        if plte:
                            v.wait_ge(cT.h, plte)
                        cV.tick(v.tensor_scalar(latnb[:], latsl, mv[:, 0:1], lnr[:, 2:3],
                                                Op.subtract, Op.mult))
                    self.V(_)
                    prev_lat_free[buf] = cV.n
                    v_latn = cV.n
                    plse = prev_lt_sevac_v
                    def _(tns):
                        tns.wait_ge(cV.h, max(v_latn, plse))
                        for lc in range(2):
                            self._lmm = tns.matmul(lt_ps[:, lc * 128:(lc + 1) * 128],
                                       latnb[:, lc * 128:(lc + 1) * 128], self.s_ident[:],
                                       start=True, stop=True, is_transpose=True,
                                       skip_group_check=True)
                        cT.tick(self._lmm)
                    self.T(_)
                    prev_lt_evac = cT.n
                    t_lt = cT.n
                    def _(v):
                        v.wait_ge(cT.h, t_lt)
                        cV.tick(v.tensor_copy(
                            bass.AP(self.latnT, ti * 128, [[2 * TP, 128], [TP, 2], [1, 128]]),
                            lt_ps[:]))
                    self.V(_)
                    prev_lt_sevac_v = cV.n
                if self.debug:
                    def _(y):
                        y.wait_ge(cV.h, cV.n)
                        dY.tick(y.dma_start(self.dbg["d_latnT"].ap()[:], self.latnT[:]), 16)
                    self.Y(_)

    def p2_groups(self):
        nc = self.nc
        cV = self.cV
        with ExitStack() as p2:
            sb = lambda n, shp, dt: p2.enter_context(nc.sbuf_tensor(n, shp, dt))
            B = {}
            B["ktg"] = sb("ktg", [128, 2 * TP], BF16)
            B["vg"] = sb("vg", [128, 32 * 260], BF16)
            B["kdiag"] = sb("kdiag", [128, 2 * 2048], BF16)
            B["vdiag"] = sb("vdiag", [128, 16 * 260], BF16)
            B["qtg"] = sb("qtg", [128, 2 * TO], BF16)
            B["PT"] = sb("PT", [128, 2 * 2048], BF16)
            B["PTtri"] = sb("PTtri", [128, 2048], BF16)
            B["denb"] = sb("denb", [65, 8 * 512], F32)
            B["denm"] = sb("denm", [4, 2 * 512], F32)
            B["Dp"] = sb("Dp", [128, 4 * 512], F32)
            B["ostg"] = sb("ostg", [64, 512], BF16)
            B["wuk_b"] = sb("wuk_b", [128, 2 * 256], BF16)
            B["wuv_b"] = sb("wuv_b", [128, 2 * 256], BF16)
            B["wuk_f"] = sb("wuk_f", [128, 2 * 2 * 256], F32)
            B["wuv_f"] = sb("wuv_f", [128, 2 * 2 * 256], F32)
            B["gsc"] = sb("gsc", [128, 2], F32)
            B["bWk_sb"] = sb("bWk_sb", [128, 2], F32)
            B["bWv_sb"] = sb("bWv_sb", [128, 2], F32)
            B["bWvrow"] = sb("bWvrow", [1, 256], F32)
            B["bWvbc"] = sb("bWvbc", [128, 256], F32)
            B["ckt"] = sb("ckt", [128, 2 * 512], F32)
            B["skt"] = sb("skt", [128, 2 * 512], F32)
            B["kraw"] = sb("kraw", [128, 512], BF16)
            B["k1b"] = sb("k1b", [128, 512], BF16)
            B["k2b"] = sb("k2b", [128, 512], BF16)
            B["ksq"] = sb("ksq", [128, 512], BF16)
            B["bias_fb"] = sb("bias_fb", [1, 32], BF16)

            def _(v):
                cV.tick(v.memset(B["PTtri"][:], 0.0))
            self.V(_)
            def _(v):
                v.wait_ge(self.dY.h, self.consts_ld)
                cV.tick(v.tensor_scalar(B["gsc"][:], self.s_g2[:], INVS8, None, Op.mult))
            self.V(_)
            B["gsc_ready"] = cV.n

            self.w_mark = [None] * 4
            def _(sc):
                self.emit_wload(sc, 0, B, self.dS)
            self.S(_)

            for g in range(4):
                self.emit_group(g, B)

    def emit_pvp(self, tns, pd):
        cT, cV = self.cT, self.cV
        qb, hl = pd["qb"], pd["hl"]
        tns.wait_ge(self.dG.h, self._vdiag_done)
        for u in range(2):
            tns.wait_ge(cV.h, pd["tri_vt"][u])
            for j2 in range(2):
                jj = 2 * u + j2
                self._lmm = tns.matmul(
                    pd["pvsl"],
                    self._vdiag[:, (qb * 4 + jj) * 260 + hl * 65:
                                (qb * 4 + jj) * 260 + hl * 65 + 65],
                    self._PTtri[:, jj * 512:(jj + 1) * 512],
                    start=False, stop=(u == 1 and j2 == 1),
                    skip_group_check=True)
            cT.tick(self._lmm)

    def emit_wload(self, eng, g, B, d):
        par = g % 2
        d.tick(eng.dma_start(
            bass.AP(B["wuk_f"], par * 512, [[1024, 128], [256, 2], [1, 256]]),
            bass.AP(self.wuk, g * 256, [[D, 128], [128 * D, 2], [1, 256]])), 16)
        d.tick(eng.dma_start(
            bass.AP(B["wuv_f"], par * 512, [[1024, 128], [256, 2], [1, 256]]),
            bass.AP(self.wuv, g * 256, [[D, 128], [128 * D, 2], [1, 256]])), 16)
        self.w_mark[g] = (d, d.n)

    def emit_group(self, g, B):
        nc = self.nc
        cT, cV, cS, cG = self.cT, self.cV, self.cS, self.cG
        dY, dS, dG = self.dY, self.dS, self.dG
        ktg, vg, kdiag, vdiag, qtg = B["ktg"], B["vg"], B["kdiag"], B["vdiag"], B["qtg"]
        PT, PTtri, Dp, ostg = B["PT"], B["PTtri"], B["Dp"], B["ostg"]
        self._vdiag, self._PTtri = B["vdiag"], B["PTtri"]
        wuk_b, wuv_b, wuk_f, wuv_f = B["wuk_b"], B["wuv_b"], B["wuk_f"], B["wuv_f"]
        gsc, bWk_sb, bWv_sb = B["gsc"], B["bWk_sb"], B["bWv_sb"]
        bWvrow, bWvbc = B["bWvrow"], B["bWvbc"]
        ckt, skt = B["ckt"], B["skt"]
        kraw, k1b, k2b, ksq = B["kraw"], B["k1b"], B["k2b"], B["ksq"]
        bias_fb = B["bias_fb"]

        self.barrier()

        wpar = g % 2
        def _(y):
            y.wait_ge(cV.h, cV.n)
            dY.tick(y.dma_start(qtg[:], self.qt_d.ap()[:, 2 * g * TO:(2 * g + 2) * TO]), 16)
        self.Y(_)
        wld = dY.n
        wd, wm = self.w_mark[g]
        def _(v):
            v.wait_ge(wd.h, wm)
            v.wait_ge(cV.h, B["gsc_ready"])
            for lc in range(2):
                v.tensor_scalar(wuk_b[:, lc * 256:(lc + 1) * 256],
                                wuk_f[:, wpar * 512 + lc * 256:wpar * 512 + (lc + 1) * 256],
                                gsc[:, lc:lc + 1], None, Op.mult)
                cV.tick(v.tensor_scalar(wuv_b[:, lc * 256:(lc + 1) * 256],
                                        wuv_f[:, wpar * 512 + lc * 256:wpar * 512 + (lc + 1) * 256],
                                        self.s_g2[:, lc:lc + 1], None, Op.mult))
        self.V(_)
        w_ready = cV.n

        with ExitStack() as pb:
            bW_ps = pb.enter_context(nc.psum_tensor("bW_ps%d" % g, [128, 4 * 512], F32))
            bc_ps = pb.enter_context(nc.psum_tensor("bc_ps%d" % g, [128, 256], F32))
            def _(tns):
                tns.wait_ge(cV.h, w_ready)
                for half in range(2):
                    for lc in range(2):
                        self._lmm = tns.matmul(bW_ps[:, half * 512:half * 512 + 1],
                                   wuk_f[:, wpar * 512 + lc * 256 + half * 128:
                                         wpar * 512 + lc * 256 + (half + 1) * 128],
                                   self.s_b2[:, lc:lc + 1],
                                   start=(lc == 0), stop=(lc == 1))
                        self._lmm = tns.matmul(bW_ps[:, (2 + half) * 512:(2 + half) * 512 + 1],
                                   wuv_f[:, wpar * 512 + lc * 256 + half * 128:
                                         wpar * 512 + lc * 256 + (half + 1) * 128],
                                   self.s_b2[:, lc:lc + 1],
                                   start=(lc == 0), stop=(lc == 1))
                cT.tick(self._lmm)
            self.T(_)
            t_bw = cT.n
            def _(v):
                v.wait_ge(cT.h, t_bw)
                for half in range(2):
                    v.tensor_scalar(bWk_sb[:, half:half + 1],
                                    bW_ps[:, half * 512:half * 512 + 1],
                                    INVS8, None, Op.mult)
                    cV.tick(v.tensor_copy(bWv_sb[:, half:half + 1],
                                          bW_ps[:, (2 + half) * 512:(2 + half) * 512 + 1]))
            self.V(_)
            v_bw = cV.n
            def _(gp):
                gp.wait_ge(cV.h, v_bw)
                dG.tick(gp.dma_start(bWvrow[0:1, 0:128], bWv_sb[:, 0:1]), 16)
                dG.tick(gp.dma_start(bWvrow[0:1, 128:256], bWv_sb[:, 1:2]), 16)
            self.G(_)
            g_bw = dG.n
            def _(tns):
                tns.wait_ge(dG.h, g_bw)
                tns.wait_ge(dY.h, self.consts_ld)
                cT.tick(tns.matmul(bc_ps[:], self.s_ones1f[:], bWvrow[:],
                                   start=True, stop=True))
            self.T(_)
            t_bc = cT.n
            def _(v):
                v.wait_ge(cT.h, t_bc)
                cV.tick(v.tensor_copy(bWvbc[:], bc_ps[:]))
            self.V(_)
        bWvbc_ready = cV.n

        with ExitStack() as pk:
            psx = lambda n, shp, dt: pk.enter_context(nc.psum_tensor(n + str(g), shp, dt))
            kq_ps = psx("kq_ps", [128, 2 * 512], F32)
            ksw_ps = psx("ksw_ps", [128, 512], F32)
            nk_ps = psx("nk_ps", [1, 512], F32)
            vp_ps = psx("vp_ps", [128, 2 * 512], F32)

            def _(v):
                cV.tick(v.memset(bass.AP(vg, 64, [[32 * 260, 128], [260, 32], [65, 4]]), 1.0))
            self.V(_)
            ones_set = cV.n

            tab_ld = [0] * 8
            def _(y):
                for tb in range(2):
                    dY.tick(y.dma_start(ckt[:, tb * 512:(tb + 1) * 512],
                                        self.cosk.ap()[:, tb * 512:(tb + 1) * 512]), 16)
                    dY.tick(y.dma_start(skt[:, tb * 512:(tb + 1) * 512],
                                        self.sink.ap()[:, tb * 512:(tb + 1) * 512]), 16)
                    tab_ld[tb] = dY.n
            self.Y(_)

            prev_kraw_perm = 0
            prev_kq_free = [0, 0]
            for tb in range(8):
                tbuf = tb % 2
                for kc in range(2):
                    ksl = kq_ps[:, kc * 512:(kc + 1) * 512]
                    pkf = prev_kq_free[kc]
                    def _(tns):
                        tns.wait_ge(cV.h, max(w_ready, self.latnT_ready_v))
                        if pkf:
                            tns.wait_ge(cV.h, pkf)
                        for lc in range(2):
                            self._lmm = tns.matmul(ksl,
                                       wuk_b[:, lc * 256 + kc * 128:lc * 256 + (kc + 1) * 128],
                                       self.latnT[:, lc * TP + tb * 512:lc * TP + (tb + 1) * 512],
                                       start=(lc == 0), stop=(lc == 1))
                        cT.tick(self._lmm)
                    self.T(_)
                    t_k = cT.n
                    pkp = prev_kraw_perm
                    tl = tab_ld[tb]
                    def _(v):
                        v.wait_ge(cT.h, t_k)
                        v.wait_ge(dY.h, tl)
                        v.wait_ge(cV.h, bWvbc_ready)
                        if pkp:
                            v.wait_ge(cT.h, pkp)
                        cV.tick(v.tensor_scalar(kraw[:], ksl, bWk_sb[:, kc:kc + 1],
                                                None, Op.add))
                    self.V(_)
                    v_kraw = cV.n
                    prev_kq_free[kc] = v_kraw
                    def _(tns):
                        tns.wait_ge(cV.h, v_kraw)
                        cT.tick(tns.matmul(ksw_ps[:], self.s_perm[:], kraw[:],
                                           start=True, stop=True))
                    self.T(_)
                    prev_kraw_perm = cT.n
                    t_sw = cT.n
                    def _(v):
                        v.wait_ge(cT.h, t_sw)
                        cslice = ckt[:, tbuf * 512:(tbuf + 1) * 512]
                        sslice = skt[:, tbuf * 512:(tbuf + 1) * 512]
                        v.tensor_tensor(k1b[:], kraw[:], cslice, Op.mult)
                        v.tensor_tensor(k2b[:], ksw_ps[:], sslice, Op.mult)
                        v.drain()
                        v.tensor_tensor(ktg[:, kc * TP + tb * 512:kc * TP + (tb + 1) * 512],
                                        k1b[:], k2b[:], Op.add)
                        cV.tick(v.tensor_tensor(ksq[:], kraw[:], kraw[:], Op.mult))
                    self.V(_)
                    v_rope = cV.n
                    def _(tns):
                        tns.wait_ge(cV.h, v_rope)
                        cT.tick(tns.matmul(nk_ps[:], self.s_ones128[:], ksq[:],
                                           start=True, stop=True))
                    self.T(_)
                    t_n = cT.n
                    def _(v):
                        v.wait_ge(cT.h, t_n)
                        cV.tick(v.reduce_max(self.kn2m[0:1, kc * 8 + tb:kc * 8 + tb + 1],
                                             nk_ps[:], axis=AX.X))
                    self.V(_)
                if tb + 2 < 8:
                    vn = cV.n
                    def _(y):
                        y.wait_ge(cV.h, vn)
                        dY.tick(y.dma_start(ckt[:, tbuf * 512:(tbuf + 1) * 512],
                                            self.cosk.ap()[:, (tb + 2) * 512:(tb + 3) * 512]), 16)
                        dY.tick(y.dma_start(skt[:, tbuf * 512:(tbuf + 1) * 512],
                                            self.sink.ap()[:, (tb + 2) * 512:(tb + 3) * 512]), 16)
                        tab_ld[tb + 2] = dY.n
                    self.Y(_)
            prev_norm = cV.n

            def _(y):
                y.wait_ge(cV.h, prev_norm)
                dY.tick(y.dma_start(self.kd_d.ap()[:, :], ktg[:]), 16)
            self.Y(_)
            k_spill = dY.n
            from concourse.bass_core_rust import IndirectOffsetOnAxis
            def _(gp):
                gp.wait_ge(dY.h, k_spill)
                rko = self.r_ko
                for qb in range(4):
                    gp.reg_load(rko, self.s_dko[0:1, qb:qb + 1])
                    dG.tick(gp.dma_start(
                        bass.AP(kdiag, qb * 512, [[4096, 128], [2048, 2], [1, 512]]),
                        bass.AP(self.kd_d, rko, [[2 * TP, 128], [TP, 2], [1, 512]])), 16)
            self.G(_)
            kdiag_done = dG.n

            with ExitStack() as pbias:
                b_ps = pbias.enter_context(nc.psum_tensor("b_ps%d" % g, [128, 32], F32))
                def _(v):
                    v.wait_ge(cV.h, prev_norm)
                    v.drain()
                    v.reduce_max(self.kn2m[0:1, 16:17], self.kn2m[0:1, 0:16], axis=AX.X)
                    v.drain()
                    v.tensor_scalar(self.kn2m[0:1, 17:18], self.kn2m[0:1, 16:17],
                                    self.qn2m[0:1, 32:33], -0.5, Op.add, Op.mult)
                    v.drain()
                    cV.tick(v.tensor_scalar(bias_fb[:], self.s_kill[:],
                                            self.kn2m[0:1, 17:18], None, Op.add))
                self.V(_)
                v_bf = cV.n
                def _(tns):
                    tns.wait_ge(cV.h, v_bf)
                    cT.tick(tns.matmul(b_ps[:], self.s_ones1b[:], bias_fb[:],
                                       start=True, stop=True))
                self.T(_)
                t_b = cT.n
                def _(v):
                    v.wait_ge(cT.h, t_b)
                    cV.tick(v.tensor_copy(self.bias_all[:], b_ps[:]))
                self.V(_)
            bias_ready = cV.n

            prev_vevac = [0, 0]
            for ti in range(32):
                vbuf = ti % 2
                vsl = vp_ps[:, vbuf * 512:vbuf * 512 + 256]
                pve = prev_vevac[vbuf]
                def _(tns):
                    tns.wait_ge(cV.h, max(w_ready, self.latnT_ready_v))
                    if pve:
                        tns.wait_ge(cV.h, pve)
                    for lc in range(2):
                        self._lmm = tns.matmul(vsl,
                                   self.latnT[:, lc * TP + ti * 128:lc * TP + (ti + 1) * 128],
                                   wuv_b[:, lc * 256:(lc + 1) * 256],
                                   start=(lc == 0), stop=(lc == 1))
                    cT.tick(self._lmm)
                self.T(_)
                t_v = cT.n
                def _(v):
                    v.wait_ge(cT.h, t_v)
                    v.wait_ge(cV.h, max(bWvbc_ready, ones_set))
                    cV.tick(v.tensor_tensor(
                        bass.AP(vg, ti * 260, [[32 * 260, 128], [65, 4], [1, 64]]),
                        vsl, bWvbc[:], Op.add))
                self.V(_)
                prev_vevac[vbuf] = cV.n
            v_vdone = cV.n

            def _(y):
                y.wait_ge(cV.h, v_vdone)
                dY.tick(y.dma_start(
                    bass.AP(self.vd_d, 0, [[260, 128], [128 * 260, 32], [1, 260]]),
                    vg[:]), 16)
            self.Y(_)
            v_spill = dY.n
            def _(gp):
                gp.wait_ge(dY.h, v_spill)
                for t in range(16):
                    dG.tick(gp.indirect_dma_start(
                        vdiag[:, t * 260:(t + 1) * 260], None,
                        self.vd_d.ap()[:, :],
                        IndirectOffsetOnAxis(self.s_vidx[:, t:t + 1], 0)), 16)
            self.G(_)
            diag_done = dG.n
            self._vdiag_done = diag_done

        if self.debug and g == 0:
            def _(y):
                y.wait_ge(cV.h, cV.n)
                y.wait_ge(dG.h, dG.n)
                dY.tick(y.dma_start(self.dbg["d_kt"].ap()[:], ktg[:]), 16)
                dY.tick(y.dma_start(self.dbg["d_vg"].ap()[:], vg[:]), 16)
                dY.tick(y.dma_start(self.dbg["d_bias"].ap()[:], self.bias_all[:]), 16)
                dY.tick(y.dma_start(self.dbg["d_qt"].ap()[:], qtg[:]), 16)
            self.Y(_)

        with ExitStack() as pa:
            scb = [pa.enter_context(nc.psum_tensor("sc%d_%d" % (c, g), [128, 1024], F32))
                   for c in range(3)]
            pv_ps = pa.enter_context(nc.psum_tensor("pv_ps%d" % g, [128, 2 * 512], F32))
            denb, denm = B["denb"], B["denm"]
            pv_evac = [0, 0]
            ost_dma = self._ost_dma
            gather_mark = [0, 0]
            dp_free = [0, 0]
            pend = None
            norm_pend = None
            prev_et = None
            for qi, qb in enumerate([3, 2, 1, 0]):
                par = qb % 2
                nss = 2 * SS[qb]
                for hl in range(4):
                    kc, p = hl // 2, hl % 2
                    pr = p * 64
                    gc = 2 * g + kc
                    pvb = hl % 2
                    pvsl = pv_ps[0:65, pvb * 512:pvb * 512 + 512]
                    T0, S0, V0 = cT.n, cS.n, cV.n
                    if pend is not None:
                        assert T0 == pend["T0_next"], (T0, pend["T0_next"])
                    npend = 2 if pend is not None else 0
                    ops = [("sc", 0), ("sc", 1), ("sc", 2), ("pv", 0), ("sc", 3)]
                    if npend:
                        ops.append(("pvp", 0))
                    for i in range(1, nss - 2):
                        ops.append(("pv", i))
                        if i + 3 < nss:
                            ops.append(("sc", i + 3))
                    sc_tick = [0] * nss
                    pv_tick = [0] * nss
                    t = T0
                    for op, a in ops:
                        t += (2 if op == "pvp" else 1)
                        if op == "sc":
                            sc_tick[a] = t
                        elif op == "pv":
                            pv_tick[a] = t
                    e_tick = []
                    acc = S0
                    for i in range(nss):
                        acc += (2 if i >= nss - 2 else 1)
                        e_tick.append(acc)
                    tri_vt = [V0 + 1, V0 + 2]

                    pe0 = pv_evac[pvb]
                    pd = pend
                    pet = prev_et
                    pnss = len(pet) if pet is not None else 0
                    def _(tns):
                        tns.wait_ge(cV.h, bias_ready)
                        tns.wait_ge(dY.h, wld)
                        if pe0:
                            tns.wait_ge(cV.h, pe0)

                        def scores(i):
                            dst = scb[i % 3]
                            if i < 3 and pet is not None:
                                j = pnss - 1 - ((pnss - 1 - i) % 3)
                                tns.wait_ge(cS.h, pet[j])
                            if i == nss - 2:
                                tns.wait_ge(dG.h, kdiag_done)
                            if i >= 3:
                                tns.wait_ge(cS.h, e_tick[i - 3])
                            for j2 in range(2):
                                j = 2 * i + j2
                                if i >= nss - 2:
                                    jj = j - 2 * (nss - 2)
                                    lhs = kdiag[pr:pr + 64,
                                                kc * 2048 + qb * 512 + jj * 128:
                                                kc * 2048 + qb * 512 + (jj + 1) * 128]
                                else:
                                    lhs = ktg[pr:pr + 64,
                                              kc * TP + j * 128:kc * TP + (j + 1) * 128]
                                self._lmm = tns.matmul(
                                    dst[:, j2 * 512:(j2 + 1) * 512], lhs,
                                    qtg[pr:pr + 64, kc * TO + qb * 512:kc * TO + (qb + 1) * 512],
                                    start=True, stop=True)
                            cT.tick(self._lmm)

                        def pv(i):
                            tns.wait_ge(cS.h, e_tick[i])
                            for j2 in range(2):
                                j = 2 * i + j2
                                src = PT[:, (i % 2) * 1024 + j2 * 512:
                                         (i % 2) * 1024 + (j2 + 1) * 512]
                                vsrc = vg[:, j * 260 + hl * 65:j * 260 + hl * 65 + 65]
                                self._lmm = tns.matmul(pvsl, vsrc, src,
                                                       start=(i == 0 and j2 == 0),
                                                       stop=False,
                                                       skip_group_check=True)
                            cT.tick(self._lmm)

                        for op, a in ops:
                            if op == "sc":
                                scores(a)
                            elif op == "pv":
                                pv(a)
                            else:
                                self.emit_pvp(tns, pd)
                    self.T(_)
                    assert cT.n == T0 + 2 * nss - 2 + npend, (cT.n, T0, nss)

                    def _(sc):
                        for i in range(nss):
                            sc.wait_ge(cT.h, sc_tick[i])
                            if i < nss - 2:
                                if i >= 2:
                                    sc.wait_ge(cT.h, pv_tick[i - 2])
                                bias_ap = self.bias_all[:, qb * 8 + i // 2:
                                                        qb * 8 + i // 2 + 1]
                                cS.tick(sc.activation(
                                    PT[:, (i % 2) * 1024:(i % 2) * 1024 + 1024],
                                    scb[i % 3][:], AF.Exp, bias=bias_ap))
                            else:
                                sc.wait_ge(cT.h, T0 + 5 + npend)
                                bias_ap = self.bias_all[:, qb * 8 + SS[qb] - 1:
                                                        qb * 8 + SS[qb]]
                                u = i - (nss - 2)
                                for j2 in range(2):
                                    jg = 2 * u + j2
                                    lo = jg * 512 + 128 * jg
                                    hi = (jg + 1) * 512
                                    cS.tick(sc.activation(
                                        PTtri[:, lo:hi],
                                        scb[i % 3][:, j2 * 512 + 128 * jg:(j2 + 1) * 512],
                                        AF.Exp, bias=bias_ap))
                    self.S(_)
                    assert cS.n == e_tick[-1]

                    is_last = (qi == 3 and hl == 3)
                    po = ost_dma
                    gm = gather_mark[par]
                    et0, et1 = e_tick[nss - 2], e_tick[nss - 1]
                    def _(v):
                        for u in range(2):
                            v.wait_ge(cS.h, (et0, et1)[u])
                            for j2 in range(2):
                                jg = 2 * u + j2
                                lo = jg * 512 + 128 * jg
                                ins = v.tensor_tensor(PTtri[:, lo:lo + 128],
                                                      PTtri[:, lo:lo + 128],
                                                      self.s_tri[:], Op.mult)
                            cV.tick(ins)
                        v.wait_ge(cT.h, cT.n + (2 if is_last else 7))
                        if gm:
                            v.wait_ge(dG.h, gm)
                        if p == 1 and po:
                            v.wait_ge(dY.h, po)
                        v.tensor_copy(denb[64:65, (par * 4 + hl) * 512:(par * 4 + hl + 1) * 512],
                                      pvsl[64:65, :])
                        if p == 0:
                            dstO = self.otT[0:64, gc * TO + qb * 512:gc * TO + (qb + 1) * 512]
                        else:
                            dstO = ostg[0:64, 0:512]
                        cV.tick(v.tensor_copy(dstO, pvsl[0:64, :]))
                    self.V(_)
                    assert cV.n == V0 + 3
                    pv_evac[pvb] = cV.n
                    pend = {"qb": qb, "hl": hl, "pvsl": pvsl, "tri_vt": tri_vt,
                            "T0_next": cT.n}
                    prev_et = e_tick
                    if p == 1:
                        v_o = cV.n
                        def _(y):
                            y.wait_ge(cV.h, v_o)
                            dY.tick(y.dma_start(
                                self.otT[64:128, gc * TO + qb * 512:gc * TO + (qb + 1) * 512],
                                ostg[0:64, 0:512]), 16)
                        self.Y(_)
                        ost_dma = dY.n
                        if qi == 0 and hl == 3 and g < 3:
                            def _(y):
                                self.emit_wload(y, g + 1, B, dY)
                            self.Y(_)
                    if hl == 0 and norm_pend is not None:
                        norm_pend()
                        norm_pend = None

                if qi == 3:
                    pd = pend
                    def _(tns):
                        self.emit_pvp(tns, pd)
                    self.T(_)
                    pend = None

                def emit_norm(qb=qb, par=par, v_den=cV.n, od2=ost_dma):
                    def _(gp):
                        gp.wait_ge(cV.h, v_den)
                        dG.tick(gp.dma_start(
                            denm[0:4, par * 512:(par + 1) * 512],
                            bass.AP(denb, 64 * 4096 + par * 2048,
                                    [[4096, 1], [512, 4], [1, 512]])), 16)
                    self.G(_)
                    gather_mark[par] = dG.n
                    gm2 = gather_mark[par]
                    def _(v):
                        v.wait_ge(dG.h, gm2)
                        cV.tick(v.reciprocal(denm[0:4, par * 512:(par + 1) * 512],
                                             denm[0:4, par * 512:(par + 1) * 512]))
                    self.V(_)
                    v_rec = cV.n
                    def _(gp):
                        gp.wait_ge(cV.h, v_rec)
                        for hl in range(4):
                            kc, p = hl // 2, hl % 2
                            dG.tick(gp.dma_start(
                                bass.AP(Dp, p * 64 * 2048 + (par * 2 + kc) * 512,
                                        [[2048, 64], [1, 512]]),
                                bass.AP(denm, hl * 1024 + par * 512,
                                        [[1024, 1], [0, 64], [1, 512]])), 16)
                    self.G(_)
                    def _(gp):
                        if od2:
                            gp.wait_ge(dY.h, od2)
                        for kc in range(2):
                            gc = 2 * g + kc
                            sl = self.otT[0:128,
                                          gc * TO + qb * 512:gc * TO + (qb + 1) * 512]
                            ins = gp.tensor_tensor(
                                sl, sl,
                                bass.AP(Dp, (par * 2 + kc) * 512, [[2048, 128], [1, 512]]),
                                Op.mult)
                        cG.tick(ins)
                    self.G(_)
                if qi == 3:
                    emit_norm()
                else:
                    norm_pend = emit_norm
            self._ost_dma = ost_dma

        if self.debug and g == 3:
            def _(y):
                y.wait_ge(cV.h, cV.n)
                y.wait_ge(self.cG.h, self.cG.n)
                y.wait_ge(dY.h, dY.n)
                dY.tick(y.dma_start(self.dbg["d_ot"].ap()[:], self.otT[:]), 16)
            self.Y(_)

    def p3_epilogue(self):
        nc = self.nc
        cT, cV, cS = self.cT, self.cV, self.cS
        dY, dS, dG = self.dY, self.dS, self.dG
        with ExitStack() as p3:
            sb = lambda n, shp, dt: p3.enter_context(nc.sbuf_tensor(n, shp, dt))
            ps = lambda n, shp, dt: p3.enter_context(nc.psum_tensor(n, shp, dt))
            wo_b = sb("wo_b", [128, 8 * D], BF16)
            qrh_b = sb("qrh_b", [128, 8 * D], BF16)
            rs_row = sb("rs_row", [1, D], F32)
            RS = sb("RS", [128, D], F32)
            qlt = sb("qlt", [128, 2 * 1024], BF16)
            xot = sb("xot", [128, 2 * D], F32)
            st_f = sb("st_f", [128, D], F32)
            gt_f = sb("gt_f", [128, D], F32)
            gs_f = sb("gs_f", [128, D], F32)
            outst = sb("outst", [128, 2 * D], F32)
            sp_ps = ps("sp_ps", [128, 1024], F32)
            ap_ps = ps("ap_ps", [128, 1024], F32)
            rs_ps = ps("rs_ps", [128, 1024], F32)

            def _(gp):
                dG.tick(gp.dma_start(
                    bass.AP(wo_b, 0, [[8 * D, 128], [D, 8], [1, D]]),
                    bass.AP(self.wo, 0, [[D, 128], [128 * D, 8], [1, D]])), 16)
            self.G(_)
            wo_ld = dG.n
            ql_lds = [0] * 16
            xo_lds = [0] * 16
            def _(y):
                dY.tick(y.dma_start(qrh_b[:], self.qr_d.ap()[:]), 16)
                dY.tick(y.dma_start(rs_row[:], self.rs_d.ap()[:]), 16)
                for t in range(2):
                    dY.tick(y.dma_start(qlt[:, t * 1024:(t + 1) * 1024],
                                        bass.AP(self.ql_d, t * 128,
                                                [[8 * TO, 128], [TO, 8], [1, 128]])), 16)
                    ql_lds[t] = dY.n
                    dY.tick(y.dma_start(xot[:, t * D:(t + 1) * D],
                                        self.xo.ap()[t * 128:(t + 1) * 128, :]), 16)
                    xo_lds[t] = dY.n
            self.Y(_)
            rs_ld = dY.n
            def _(tns):
                tns.wait_ge(dY.h, rs_ld)
                for band in range(2):
                    self._lmm = tns.matmul(rs_ps[:, band * 512:(band + 1) * 512], self.s_ones1f[:],
                               rs_row[0:1, band * 512:(band + 1) * 512],
                               start=True, stop=True)
                cT.tick(self._lmm)
            self.T(_)
            t_rs = cT.n
            def _(v):
                v.wait_ge(cT.h, t_rs)
                cV.tick(v.tensor_copy(RS[:], rs_ps[:]))
            self.V(_)
            rs_ready = cV.n

            if self.debug:
                def _(y):
                    y.wait_ge(dY.h, rs_ld)
                    y.wait_ge(dG.h, wo_ld)
                    dY.tick(y.dma_start(self.dbg["d_ot2"].ap()[:, :], self.otT[:]), 16)
                    dY.tick(y.dma_start(self.dbg["d_wob"].ap()[:, :], wo_b[:]), 16)
                self.Y(_)
            prev_ev = 0
            out_dma = [0, 0]
            prev_silu = 0
            for ti in range(16):
                buf = ti % 2
                qld, xld = ql_lds[ti], xo_lds[ti]
                pe = prev_ev
                def _(tns):
                    tns.wait_ge(dY.h, max(qld, xld))
                    tns.wait_ge(dG.h, wo_ld)
                    if pe:
                        tns.wait_ge(cV.h, pe)
                    for c in range(8):
                        for band in range(2):
                            self._lmm = tns.matmul(sp_ps[:, band * 512:(band + 1) * 512],
                                       self.otT[:, c * TO + ti * 128:c * TO + (ti + 1) * 128],
                                       wo_b[:, c * D + band * 512:c * D + (band + 1) * 512],
                                       start=(c == 0), stop=(c == 7), skip_group_check=True)
                    for c in range(8):
                        for band in range(2):
                            self._lmm = tns.matmul(ap_ps[:, band * 512:(band + 1) * 512],
                                       qlt[:, buf * 1024 + c * 128:buf * 1024 + (c + 1) * 128],
                                       qrh_b[:, c * D + band * 512:c * D + (band + 1) * 512],
                                       start=(c == 0), stop=(c == 7), skip_group_check=True)
                    cT.tick(self._lmm)
                self.T(_)
                t_mm = cT.n
                def _(v):
                    v.wait_ge(cT.h, t_mm)
                    v.wait_ge(cV.h, rs_ready)
                    if prev_silu:
                        v.wait_ge(cS.h, prev_silu)
                    v.tensor_copy(st_f[:], sp_ps[:])
                    v.tensor_tensor(gt_f[:], ap_ps[:], RS[:], Op.mult)
                    v.drain()
                    cV.tick(v.tensor_scalar(gt_f[:], gt_f[:], self.ls_all[:, ti:ti + 1],
                                            None, Op.mult))
                self.V(_)
                v_ev = cV.n
                prev_ev = v_ev
                if ti + 2 < 16:
                    def _(y):
                        y.wait_ge(cV.h, v_ev)
                        dY.tick(y.dma_start(qlt[:, buf * 1024:(buf + 1) * 1024],
                                            bass.AP(self.ql_d, (ti + 2) * 128,
                                                    [[8 * TO, 128], [TO, 8], [1, 128]])), 16)
                        ql_lds[ti + 2] = dY.n
                    self.Y(_)
                if self.debug and ti == 0:
                    def _(y):
                        y.wait_ge(cV.h, v_ev)
                        dY.tick(y.dma_start(self.dbg["d_st"].ap()[:, :], st_f[:]), 16)
                        dY.tick(y.dma_start(self.dbg["d_gt"].ap()[:, :], gt_f[:]), 16)
                        dY.tick(y.dma_start(self.dbg["d_RS"].ap()[:, :], RS[:]), 16)
                    self.Y(_)
                    dbg_mark = dY.n
                def _(sc):
                    sc.wait_ge(cV.h, v_ev)
                    if self.debug and ti == 0:
                        sc.wait_ge(dY.h, dbg_mark)
                    cS.tick(sc.activation(gs_f[:], gt_f[:], AF.Silu))
                self.S(_)
                s_si = cS.n
                prev_silu = s_si
                if self.debug and ti == 0:
                    def _(y):
                        y.wait_ge(cS.h, s_si)
                        dY.tick(y.dma_start(self.dbg["d_gs"].ap()[:, :], gs_f[:]), 16)
                    self.Y(_)
                    gs_mark = dY.n
                    def _(v):
                        v.wait_ge(dY.h, gs_mark)
                        v.drain()
                        cV.tick(v.nop())
                    self.V(_)
                od = out_dma[buf]
                def _(v):
                    v.wait_ge(cS.h, s_si)
                    if od:
                        v.wait_ge(dY.h, od)
                    v.tensor_tensor(gs_f[:], st_f[:], gs_f[:], Op.mult)
                    v.drain()
                    cV.tick(v.tensor_tensor(outst[:, buf * D:(buf + 1) * D],
                                            xot[:, buf * D:(buf + 1) * D], gs_f[:], Op.add))
                self.V(_)
                v_out = cV.n
                def _(y):
                    y.wait_ge(cV.h, v_out)
                    dY.tick(y.dma_start(self.out.ap()[ti * 128:(ti + 1) * 128, :],
                                        outst[:, buf * D:(buf + 1) * D]), 16)
                self.Y(_)
                out_dma[buf] = dY.n
                if ti + 2 < 16:
                    def _(y):
                        y.wait_ge(cV.h, v_out)
                        dY.tick(y.dma_start(xot[:, buf * D:(buf + 1) * D],
                                            self.xo.ap()[(ti + 2) * 128:(ti + 3) * 128, :]), 16)
                        xo_lds[ti + 2] = dY.n
                    self.Y(_)
            if self.debug:
                def _(y):
                    y.wait_ge(cV.h, cV.n)
                    dY.tick(y.dma_start(self.dbg["d_ql"].ap()[:, :], qlt[:]), 16)
                    dY.tick(y.dma_start(self.dbg["d_qr"].ap()[:], qrh_b[:]), 16)
                self.Y(_)
            self.barrier()

    def finish(self):
        def _(y):
            y.wait_ge(self.dY.h, self.dY.n)
        self.Y(_)
        def _(g):
            g.wait_ge(self.dG.h, self.dG.n)
        self.G(_)


def _host_consts():
    half = 32
    pos = np.arange(TP, dtype=np.float64)
    inv = 1.0 / (10000.0 ** (np.arange(half, dtype=np.float64) / half))
    ang = pos[None, :] * inv[:, None]
    cos = np.cos(ang)
    sin = np.sin(ang)
    cosk = np.tile(cos, (4, 1)).astype(np.float32)
    sgn = np.concatenate([-np.ones(32), np.ones(32)])
    sink = (np.tile(sin, (4, 1)) * np.tile(sgn, 2)[:, None]).astype(np.float32)
    tri = (np.arange(128)[:, None] <= np.arange(128)[None, :]).astype(ml_dtypes.bfloat16)
    ident = np.eye(128, dtype=ml_dtypes.bfloat16)
    perm = np.zeros((128, 128), np.float32)
    for i in range(128):
        blk, r = i // 64, i % 64
        perm[i, blk * 64 + ((r + 32) % 64)] = 1.0
    perm = perm.astype(ml_dtypes.bfloat16)
    return cosk, sink, tri, ident, perm


_CACHE = {}


def make_inmaps(x, wq, wdkv, wuk, wuv, wo, gam, bet, rhs):
    cosk, sink, tri, ident, perm = _host_consts()
    rhsT = np.ascontiguousarray(rhs.T)
    gamma2 = np.ascontiguousarray(gam.reshape(2, 128).T)
    beta2 = np.ascontiguousarray(bet.reshape(2, 128).T)
    in_maps = []
    for core in range(8):
        b, h = core // 2, core % 2
        own = OWN_H[h]
        xo = np.ascontiguousarray(
            np.concatenate([x[b, c * 512:(c + 1) * 512, :] for c in own], axis=0))
        vidx = np.zeros((128, 16), np.uint32)
        for qb in range(4):
            for j in range(4):
                vidx[:, qb * 4 + j] = own[qb] * 512 + j * 128 + np.arange(128)
        kl = np.zeros((1, 32), np.float32)
        for qb in range(4):
            for s in range(SS[qb] - 1):
                if s >= own[qb]:
                    kl[0, qb * 8 + s] = NEG
        in_maps.append({
            "xb": np.ascontiguousarray(x[b]), "xo": xo, "wq": wq, "wdkv": wdkv,
            "wuk": wuk, "wuv": wuv, "wo": wo, "rhsT": rhsT,
            "gamma2": gamma2, "beta2": beta2, "cosk": cosk, "sink": sink,
            "cosq": np.ascontiguousarray(
                np.concatenate([cosk[:, c * 512:(c + 1) * 512] for c in own], axis=1)),
            "sinq": np.ascontiguousarray(
                np.concatenate([sink[:, c * 512:(c + 1) * 512] for c in own], axis=1)),
            "trimask": tri, "ident": ident, "perm": perm,
            "ones128": np.ones((128, 1), ml_dtypes.bfloat16),
            "ones1b": np.ones((1, 128), ml_dtypes.bfloat16),
            "ones1f": np.ones((1, 128), np.float32),
            "kill": kl,
            "dko": np.array([[c * 512 for c in own]], np.int32),
            "vidx": vidx,
        })
    return in_maps


def kernel(L0_attn_x, L0_attn_W_q, L0_attn_W_dkv, L0_attn_W_uk, L0_attn_W_uv,
           L0_attn_W_o, L0_attn_ln_gamma, L0_attn_ln_beta, L0_ffn_rhs):
    x = np.ascontiguousarray(np.asarray(L0_attn_x, dtype=np.float32))
    wq = np.ascontiguousarray(np.asarray(L0_attn_W_q, np.float32))
    wdkv = np.ascontiguousarray(np.asarray(L0_attn_W_dkv, np.float32))
    wuk = np.ascontiguousarray(np.asarray(L0_attn_W_uk, np.float32))
    wuv = np.ascontiguousarray(np.asarray(L0_attn_W_uv, np.float32))
    wo = np.ascontiguousarray(np.asarray(L0_attn_W_o, np.float32))
    gam = np.asarray(L0_attn_ln_gamma, np.float32)
    bet = np.asarray(L0_attn_ln_beta, np.float32)
    rhs = np.ascontiguousarray(np.asarray(L0_ffn_rhs, np.float32))

    nc = _CACHE.get("nc")
    if nc is None:
        nc = MK(debug=False).build()
        _CACHE["nc"] = nc
    in_maps = make_inmaps(x, wq, wdkv, wuk, wuv, wo, gam, bet, rhs)
    res = run_bass_kernel_spmd(nc, in_maps, core_ids=list(range(8)))
    out = np.zeros((4, TP, D), np.float32)
    for core in range(8):
        b, h = core // 2, core % 2
        o = np.asarray(res.results[core]["out"])
        for i, c in enumerate(OWN_H[h]):
            out[b, c * 512:(c + 1) * 512, :] = o[i * 512:(i + 1) * 512, :]
    return out
